# version 06
# speedup vs baseline: 1.6027x; 1.3370x over previous
import numpy as np
from contextlib import ExitStack

import concourse.bass as bass
import concourse.mybir as mybir
from concourse.bass_utils import run_bass_kernel_spmd

F32 = mybir.dt.float32
BF16 = mybir.dt.bfloat16
I8 = mybir.dt.int8
I32 = mybir.dt.int32
AF = mybir.ActivationFunctionType
ALU = mybir.AluOpType
AX = mybir.AxisListType

MAGIC = 1.5 * 2.0 ** 23
ENGINES = ("sync", "gpsimd", "vector", "scalar", "tensor")



class _Op:
    __slots__ = ("engine", "fn", "reads", "writes", "inc", "gid",
                 "eidx", "deps", "signaled", "cum")

    def __init__(self, engine, fn, reads, writes, inc, gid):
        self.engine = engine
        self.fn = fn
        self.reads = reads
        self.writes = writes
        self.inc = inc
        self.gid = gid
        self.eidx = -1
        self.deps = []
        self.signaled = False
        self.cum = 0


class Prog:

    def __init__(self, nc):
        self.nc = nc
        self.ops = []
        self.last_writer = {}
        self.readers = {}

    @staticmethod
    def _key(b):
        return b if isinstance(b, str) else id(b)

    def op(self, engine, fn, reads=(), writes=(), inc=1):
        rk = tuple(self._key(b) for b in reads)
        wk = tuple(self._key(b) for b in writes)
        o = _Op(engine, fn, rk, wk, inc, len(self.ops))
        deps = []
        for b in rk:
            w = self.last_writer.get(b)
            if w is not None:
                deps.append(w)
        for b in wk:
            w = self.last_writer.get(b)
            if w is not None:
                deps.append(w)
            deps.extend(self.readers.get(b, ()))
        o.deps = [d for d in set(deps)
                  if d is not o and not (d.engine == engine == "tensor")]
        self.ops.append(o)
        for b in rk:
            self.readers.setdefault(b, []).append(o)
        for b in wk:
            self.last_writer[b] = o
            self.readers[b] = []
        return o

    def dma(self, engine, fn, reads=(), writes=()):
        return self.op(engine, fn, reads, writes, inc=16)

    def emit(self, block, stack):
        for o in self.ops:
            if o.inc == 16:
                o.signaled = True
            for d in o.deps:
                d.signaled = True
        sems = {e: stack.enter_context(self.nc.semaphore(f"sem_{e}"))
                for e in ENGINES}
        per_engine = {e: [] for e in ENGINES}
        for o in self.ops:
            per_engine[o.engine].append(o)
        for e, lst in per_engine.items():
            c = 0
            for i, o in enumerate(lst):
                o.eidx = i
                if o.signaled:
                    c += o.inc
                o.cum = c

        def mk(engine):
            lst = per_engine[engine]

            def body(e):
                watermark = {x: 0 for x in ENGINES}
                for o in lst:
                    need = {}
                    for d in o.deps:
                        v = d.cum
                        if v > need.get(d.engine, 0):
                            need[d.engine] = v
                    for te, v in need.items():
                        if v > watermark[te]:
                            e.wait_ge(sems[te], v)
                            watermark[te] = v
                    inst = o.fn(e)
                    if o.signaled:
                        inst.then_inc(sems[engine], o.inc)
            return body

        block.sync(mk("sync"))
        block.gpsimd(mk("gpsimd"))
        block.vector(mk("vector"))
        block.scalar(mk("scalar"))
        block.tensor(mk("tensor"))

    def phase_barrier(self):
        lasts = {}
        for o in self.ops:
            lasts[o.engine] = o

        def mkfn():
            def fn(e):
                return e.nop()
            return fn
        barrier_ops = []
        for eng in ENGINES:
            o = self.op(eng, mkfn())
            o.deps = [d for d in lasts.values() if d is not o]
            barrier_ops.append(o)
        for b in list(self.last_writer):
            w = self.last_writer[b]
            self.last_writer[b] = next(x for x in barrier_ops
                                       if x.engine == w.engine)
        self.readers = {b: [] for b in self.readers}

    def barrier_all(self, on_bufs):
        deps = [self.last_writer[self._key(b)] for b in on_bufs
                if self._key(b) in self.last_writer]
        for d in deps:
            d.signaled = True

        def mkfn(dlist):
            def fn(e):
                return e.nop()
            return fn
        for eng in ENGINES:
            o = self.op(eng, mkfn(deps), reads=(), writes=())
            o.deps = [d for d in deps if d.engine != eng]


def sb(nc, stack, name, shape, dtype=F32):
    return stack.enter_context(nc.sbuf_tensor(name, list(shape), dtype))


def ps(nc, stack, name, shape, dtype=F32):
    return stack.enter_context(nc.psum_tensor(name, list(shape), dtype))


T = 4096
D = 1024
NCH = 8
TC = 512
NSC = 4
SC = 128
NF = 8
NH = 16
HS = 64
CLAMP = 75.0
I0 = 4096
I1 = 3072
QMAX1 = 32767.0


def _setup_consts(nc, P, stack, need_bf16_ident=True):
    c = {}
    c["ident"] = sb(nc, stack, "c_ident", (128, 128), F32)
    c["identb"] = sb(nc, stack, "c_identb", (128, 128), BF16)
    c["tril"] = sb(nc, stack, "c_tril", (128, 128), F32)
    c["ones1"] = sb(nc, stack, "c_ones1", (1, 128), F32)
    c["onesk"] = sb(nc, stack, "c_onesk", (128, 1), F32)

    def mk_ident(t, dt):
        def f1(e):
            return e.memset(t[:], 0.0)

        def f2(e):
            return e.affine_select(out=t[:], in_=t[:], compare_op=ALU.not_equal,
                                   fill=1.0, base=0, pattern=[[-1, 128]],
                                   channel_multiplier=1)
        P.op("gpsimd", f1, writes=[t])
        P.op("gpsimd", f2, reads=[t], writes=[t])
    mk_ident(c["ident"], F32)
    mk_ident(c["identb"], BF16)

    def f3(e):
        return e.memset(c["tril"][:], 0.0)

    def f4(e):
        return e.affine_select(out=c["tril"][:], in_=c["tril"][:],
                               compare_op=ALU.is_ge, fill=0.0, base=0,
                               pattern=[[1, 128]], channel_multiplier=-1)
    P.op("gpsimd", lambda e: e.memset(c["tril"][:], 1.0), writes=[c["tril"]])
    P.op("gpsimd", f4, reads=[c["tril"]], writes=[c["tril"]])
    P.op("gpsimd", lambda e: e.memset(c["ones1"][:], 1.0), writes=[c["ones1"]])
    P.op("gpsimd", lambda e: e.memset(c["onesk"][:], 1.0), writes=[c["onesk"]])
    return c


def _load_fvec(nc, P, stack, name, hbm_ap):
    t = sb(nc, stack, name, (128, 8), F32)

    def f(e):
        return e.dma_start(out=t[:], in_=hbm_ap.rearrange("(f p) -> p f", f=8),
                           allow_slow_non_contiguous=True)
    P.dma("sync", f, writes=[t])
    return t


def _transpose128(P, c, out_psum_ap, in_ap, psum_buf, in_buf, dtype=F32):
    ident = c["ident"] if dtype == F32 else c["identb"]

    def f(e):
        return e.transpose(out_psum_ap, in_ap, ident[0:in_ap.shape[0], 0:in_ap.shape[0]])
    return P.op("tensor", f, reads=[in_buf, ident], writes=[psum_buf])



def build_k1(nch=NCH, dbg=()):
    nc = bass.Bass("TRN2", target_bir_lowering=False)
    inp = {}
    for name, shape in [
        ("x0", [T, D]), ("ln_g", [D]), ("ln_b", [D]),
        ("Wx", [D, D]), ("Ww", [D, D]), ("bw", [D]),
        ("Wk", [D, D]), ("Wv", [D, D]), ("Wr", [D, D]), ("Wo", [D, D]),
        ("wg0", [I0, D]), ("wu0", [I0, D]), ("wd0", [D, I0]),
    ]:
        inp[name] = nc.dram_tensor(name, shape, F32, kind="ExternalInput").ap()
    out = {
        "x1": nc.dram_tensor("x1", [D, T], F32, kind="ExternalOutput").ap(),
        "xn1": nc.dram_tensor("xn1", [D, T], F32, kind="ExternalOutput").ap(),
        "pmx": nc.dram_tensor("pmx", [1, 1], F32, kind="ExternalOutput").ap(),
    }
    dbg_out = {}

    def tap(name, shape, dtype=F32):
        if name in dbg and name not in dbg_out:
            dbg_out[name] = nc.dram_tensor("tap_" + name, list(shape), dtype,
                                           kind="ExternalOutput").ap()
        return dbg_out.get(name)

    xq_sp = nc.dram_tensor("xq_sp", [D, T], BF16, kind="Internal").ap()
    x1p_sp = nc.dram_tensor("x1p_sp", [D, T], F32, kind="Internal").ap()
    qgT_sp = nc.dram_tensor("qgT_sp", [D, I0], BF16, kind="Internal").ap()
    quT_sp = nc.dram_tensor("quT_sp", [D, I0], BF16, kind="Internal").ap()
    qdT_sp = nc.dram_tensor("qdT_sp", [I0, D], BF16, kind="Internal").ap()
    wbf_sp = {w: nc.dram_tensor("wbf_" + w, [D, D], BF16, kind="Internal").ap()
              for w in ("Wx", "Ww", "Wk", "Wv", "Wr", "Wo")}

    P = Prog(nc)
    final_bufs = []

    with ExitStack() as st:
        C = _setup_consts(nc, P, st)
        lng = _load_fvec(nc, P, st, "lng", inp["ln_g"])
        lnb = _load_fvec(nc, P, st, "lnb", inp["ln_b"])
        bwv = _load_fvec(nc, P, st, "bwv", inp["bw"])

        pP0 = ps(nc, st, "pP0", (128, 512))
        pP1 = ps(nc, st, "pP1", (128, 512))
        pV = ps(nc, st, "pV", (128, 1024))
        pA0 = ps(nc, st, "pA0", (128, 512))
        pA1 = ps(nc, st, "pA1", (128, 512))
        pOut = ps(nc, st, "pOut", (128, 512))
        pKT = ps(nc, st, "pKT", (128, 1024), BF16)
        pVa = pV[:, 0:512]
        pVb = pV[:, 512:1024]

        inv_sw = {w: sb(nc, st, "invsw_" + w, (1, 1)) for w in ("wg0", "wu0", "wd0")}
        cf = sb(nc, st, "cf", (1, 1))
        cfB = sb(nc, st, "cfB", (128, 1))
        dfB = sb(nc, st, "dfB", (128, 1))
        df = sb(nc, st, "df", (1, 1))
        with ExitStack() as stw:
            wtL = [sb(nc, stw, f"wt{i}", (128, I0)) for i in range(2)]
            qbL = [sb(nc, stw, f"qb{i}", (128, I0), BF16) for i in range(2)]
            wt = wtL[0]
            qb = qbL[0]
            qtb = sb(nc, stw, "qtb", (128, 1024), BF16)
            acc1 = sb(nc, stw, "acc1", (128, 1))
            acc2 = sb(nc, stw, "acc2", (128, 1))
            tmp1 = sb(nc, stw, "tmp1", (128, 1))
            scal = sb(nc, stw, "scal", (1, 4))
            swB = sb(nc, stw, "swB", (128, 1))

            def prep_ternary(wname, hbm, rows, cols, outsp, out_is_down):
                ntile = rows // 128
                accs = [acc1, acc2]
                P.op("vector", lambda e: e.memset(acc1[:], 0.0), writes=[acc1])
                for it in range(ntile):
                    wtp = wtL[it % 2]
                    def fl(e, it=it, wtp=wtp):
                        return e.dma_start(out=wtp[:, 0:cols],
                                           in_=hbm[it * 128:(it + 1) * 128, :])
                    P.dma("sync", fl, writes=[wtp])
                    P.op("vector", lambda e, wtp=wtp: e.tensor_reduce(
                        tmp1[:], wtp[:, 0:cols], AX.X, ALU.add,
                        apply_absolute_value=True), reads=[wtp], writes=[tmp1])
                    a0, a1 = accs[it % 2], accs[(it + 1) % 2]
                    P.op("vector", lambda e, a0=a0, a1=a1: e.tensor_add(
                        a1[:], a0[:], tmp1[:]),
                        reads=[a0, tmp1], writes=[a1])
                accf = accs[ntile % 2]
                _transpose128(P, C, pP0[0:1, 0:128], accf[:, 0:1], pP0, accf)
                P.op("vector", lambda e: e.tensor_reduce(
                    scal[0:1, 0:1], pP0[0:1, 0:128], AX.X, ALU.add),
                    reads=[pP0], writes=[scal])
                P.op("vector", lambda e: e.tensor_scalar(
                    scal[0:1, 1:2], scal[0:1, 0:1], 1.0 / (rows * cols), 1e-5,
                    ALU.mult, ALU.max), reads=[scal], writes=[scal])
                P.op("vector", lambda e: e.reciprocal(scal[0:1, 2:3], scal[0:1, 1:2]),
                     reads=[scal], writes=[scal])
                iv = inv_sw[wname]
                P.op("vector", lambda e: e.tensor_copy(iv[:], scal[0:1, 1:2]),
                     reads=[scal], writes=[iv])
                def fb(e):
                    return e.matmul(pP1[0:128, 0:1], C["ones1"][:],
                                    scal[0:1, 2:3], start=True, stop=True)
                P.op("tensor", fb, reads=[C["ones1"], scal], writes=[pP1])
                P.op("vector", lambda e: e.tensor_copy(swB[:], pP1[0:128, 0:1]),
                     reads=[pP1], writes=[swB])
                for it in range(ntile):
                    wtp = wtL[it % 2]
                    qb = qbL[it % 2]
                    def fl(e, it=it, wtp=wtp):
                        return e.dma_start(out=wtp[:, 0:cols],
                                           in_=hbm[it * 128:(it + 1) * 128, :])
                    P.dma("sync", fl, writes=[wtp])
                    P.op("vector", lambda e, wtp=wtp: e.tensor_scalar(
                        wtp[:, 0:cols], wtp[:, 0:cols], swB[:, 0:1], MAGIC,
                        ALU.mult, ALU.add), reads=[wtp, swB], writes=[wtp])
                    P.op("vector", lambda e, wtp=wtp: e.tensor_scalar(
                        wtp[:, 0:cols], wtp[:, 0:cols], MAGIC, 1.0,
                        ALU.subtract, ALU.min), reads=[wtp], writes=[wtp])
                    P.op("vector", lambda e, wtp=wtp, qb=qb: e.tensor_scalar(
                        qb[:, 0:cols], wtp[:, 0:cols], -1.0, None, ALU.max),
                        reads=[wtp], writes=[qb])
                    ngrp = cols // 1024
                    for g in range(ngrp):
                        for b in range(8):
                            def ft(e, g=g, b=b, qb=qb):
                                return e.transpose(
                                    pKT[:, b * 128:(b + 1) * 128],
                                    qb[:, (g * 8 + b) * 128:(g * 8 + b + 1) * 128],
                                    C["identb"][:])
                            P.op("tensor", ft, reads=[qb, C["identb"]], writes=[pKT])
                        P.op("vector", lambda e: e.tensor_copy(qtb[:], pKT[:]),
                             reads=[pKT], writes=[qtb])
                        if out_is_down:
                            def fs(e, g=g, it=it):
                                dst = outsp.rearrange("(a p) H -> p a H", p=128)
                                return e.dma_start(
                                    out=dst[:, g * 8:(g + 1) * 8,
                                            it * 128:(it + 1) * 128],
                                    in_=qtb[:].rearrange("p (a j) -> p a j", a=8))
                            P.dma("sync", fs, reads=[qtb])
                        else:
                            def fs(e, it=it):
                                dst = outsp.rearrange("(a p) I -> p a I", p=128)
                                return e.dma_start(
                                    out=dst[:, :, it * 128:(it + 1) * 128],
                                    in_=qtb[:].rearrange("p (a j) -> p a j", a=8))
                            P.dma("sync", fs, reads=[qtb])

            nw = 0
            for wname in ("Wx", "Ww", "Wk", "Wv", "Wr", "Wo"):
                for it in range(8):
                    wtp = wtL[nw % 2]
                    qbp = qbL[nw % 2]
                    nw += 1
                    def flw(e, wname=wname, it=it, wtp=wtp):
                        return e.dma_start(
                            out=wtp[:, 0:1024],
                            in_=inp[wname][it * 128:(it + 1) * 128, :])
                    P.dma("sync", flw, writes=[wtp])
                    P.op("vector", lambda e, wtp=wtp, qbp=qbp: e.tensor_copy(
                        qbp[:, 0:1024], wtp[:, 0:1024]), reads=[wtp], writes=[qbp])
                    def fsw(e, wname=wname, it=it, qbp=qbp):
                        return e.dma_start(
                            out=wbf_sp[wname][it * 128:(it + 1) * 128, :],
                            in_=qbp[:, 0:1024])
                    P.dma("sync", fsw, reads=[qbp], writes=["tok_wbf"])
            prep_ternary("wg0", inp["wg0"], I0, D, qgT_sp, False)
            prep_ternary("wu0", inp["wu0"], I0, D, quT_sp, False)
            prep_ternary("wd0", inp["wd0"], D, I0, qdT_sp, True)
            P.op("vector", lambda e: e.tensor_mul(df[:], inv_sw["wg0"][:],
                                                  inv_sw["wg0"][:]),
                 reads=[inv_sw["wg0"]], writes=[df])
            P.op("vector", lambda e: e.tensor_mul(cf[:], df[:], inv_sw["wu0"][:]),
                 reads=[df, inv_sw["wu0"]], writes=[cf])
            def fb2(e):
                return e.matmul(pP1[0:128, 0:1], C["ones1"][:], cf[:],
                                start=True, stop=True)
            P.op("tensor", fb2, reads=[C["ones1"], cf], writes=[pP1])
            P.op("vector", lambda e: e.tensor_copy(cfB[:], pP1[0:128, 0:1]),
                 reads=[pP1], writes=[cfB])
            P.op("vector", lambda e: e.tensor_scalar_mul(df[:], inv_sw["wd0"][:],
                                                         1.0 / 127.0),
                 reads=[inv_sw["wd0"]], writes=[df])

            def fb3(e):
                return e.matmul(pP1[0:128, 0:1], C["ones1"][:], df[:],
                                start=True, stop=True)
            P.op("tensor", fb3, reads=[C["ones1"], df], writes=[pP1])
            P.op("vector", lambda e: e.tensor_copy(dfB[:], pP1[0:128, 0:1]),
                 reads=[pP1], writes=[dfB])

        mq_tm = sb(nc, st, "mq_tm", (128, NSC * NCH))
        pmx128 = sb(nc, st, "pmx128", (128, 1))
        pmx128b = sb(nc, st, "pmx128b", (128, 1))
        pmxr = sb(nc, st, "pmxr", (1, 1))
        P.phase_barrier()
        stA = st.enter_context(ExitStack())

        W = {}
        for wname in ("Ww", "Wk", "Wv", "Wr"):
            t = sb(nc, stA, "sb_" + wname, (128, NF * 1024), BF16)
            for ic in range(NF):
                def fw(e, wname=wname, ic=ic, t=t):
                    return e.dma_start(
                        out=t[:, ic * 1024:(ic + 1) * 1024],
                        in_=wbf_sp[wname][ic * 128:(ic + 1) * 128, :])
                P.dma("sync", fw, reads=["tok_wbf"], writes=[t])
            W[wname] = t
        W_xo = sb(nc, stA, "sb_Wxo", (128, NF * 1024), BF16)
        W["Wx"] = W_xo
        W["Wo"] = W_xo

        def load_wxo(which):
            for ic in range(NF):
                def fw(e, which=which, ic=ic):
                    return e.dma_start(
                        out=W_xo[:, ic * 1024:(ic + 1) * 1024],
                        in_=wbf_sp[which][ic * 128:(ic + 1) * 128, :])
                P.dma("sync", fw, reads=["tok_wbf"], writes=[W_xo])

        def wtile(wname, ic, oc):
            return W[wname][:, ic * 1024 + oc * 128: ic * 1024 + (oc + 1) * 128]

        x0tm = sb(nc, stA, "x0tm", (128, NSC * 1024))
        xn0tm = sb(nc, stA, "xn0tm", (128, NSC * 1024))
        xqTM = sb(nc, stA, "xqTM", (128, NSC * 1024), BF16)

        stat = {k: [sb(nc, stA, f"st_{k}{s}", (128, 1)) for s in range(NSC)]
                for k in ("s1", "ss", "mu", "rstd", "mu2", "rstd2", "mq", "sx", "vt")}
        xn2FM = sb(nc, stA, "xn2FM", (128, NF * TC), BF16)
        x0FM = sb(nc, stA, "x0FM", (128, NF * TC))
        vTM = sb(nc, stA, "vTM", (128, NSC * 1024), BF16)
        xqFM = vTM
        z1FM = xn0tm[:].bitcast(BF16)
        outFM = xqTM
        x1pFM = x0tm
        kfiL = [sb(nc, stA, f"kfi{i}", (128, TC)) for i in range(2)]
        rfiL = [sb(nc, stA, f"rfi{i}", (128, TC)) for i in range(2)]
        zfiL = [sb(nc, stA, f"zfi{i}", (128, TC)) for i in range(2)]
        tfaL = [sb(nc, stA, f"tfa{i}", (128, TC)) for i in range(2)]
        tfbL = [sb(nc, stA, f"tfb{i}", (128, TC)) for i in range(2)]
        spfiL = [sb(nc, stA, f"spfi{i}", (128, TC)) for i in range(2)]
        DfiL = [sb(nc, stA, f"Dfi{i}", (128, TC)) for i in range(2)]
        eagL = [sb(nc, stA, f"eag{i}", (128, TC)) for i in range(2)]
        exb = {k: sb(nc, stA, "ex_" + k, (128, TC), BF16)
               for k in ("r16", "kdiv", "k16", "r32", "k32", "r64", "k64",
                         "rhat", "khat")}
        dec4 = sb(nc, stA, "dec4", (128, NSC))
        Asb = [sb(nc, stA, f"Asb{q}", (128, 128), BF16) for q in (0, 1, 2, 3)]
        kTsbL = [sb(nc, stA, f"kTsb{i}", (128, 128), BF16) for i in range(2)]
        sSt = [sb(nc, stA, f"sSt{f}", (128, HS)) for f in range(NF)]
        sSt2 = [sb(nc, stA, f"sSt2{f}", (128, HS)) for f in range(NF)]
        sStB = [sb(nc, stA, f"sStB{f}", (128, HS), BF16) for f in range(NF)]
        state_cur = list(sSt)
        state_oth = list(sSt2)

        for f in range(NF):
            P.op("vector", lambda e, f=f: e.memset(sSt[f][:], 0.0), writes=[sSt[f]])
            P.op("vector", lambda e, f=f: e.memset(sStB[f][:], 0.0), writes=[sStB[f]])
        P.op("vector", lambda e: e.memset(pmx128[:], 0.0), writes=[pmx128])
        P.op("vector", lambda e: e.memset(pmx128b[:], 0.0), writes=[pmx128b])
        pmx_pp = [pmx128, pmx128b]
        pmx_n = [0]

        def proj_oc(wname, src, srcbuf, oc, psum, extra_reads=()):
            for ic in range(NF):
                def fm(e, ic=ic, oc=oc, wname=wname):
                    return e.matmul(psum[:, 0:TC], wtile(wname, ic, oc),
                                    src[:, ic * TC:(ic + 1) * TC],
                                    start=(ic == 0), stop=(ic == NF - 1))
                P.op("tensor", fm, reads=[W[wname], srcbuf, *extra_reads],
                     writes=[psum])

        for c in range(nch):
            for sc in range(NSC):
                def fl(e, c=c, sc=sc):
                    return e.dma_start(
                        out=x0tm[:, sc * 1024:(sc + 1) * 1024],
                        in_=inp["x0"][c * TC + sc * SC: c * TC + (sc + 1) * SC, :])
                P.dma("sync", fl, writes=[x0tm])
            for sc in range(NSC):
                xsl = x0tm[:, sc * 1024:(sc + 1) * 1024]

                def ops_ln(xbuf, xap, mu_l, rstd_l, sc=sc):
                    s1 = stat["s1"][sc]
                    ss = stat["ss"][sc]
                    vt = stat["vt"][sc]
                    mu_t = mu_l[sc]
                    rstd_t = rstd_l[sc]
                    P.op("vector", lambda e: e.tensor_reduce(
                        s1[:], xap, AX.X, ALU.add), reads=[xbuf], writes=[s1])
                    P.op("scalar", lambda e: e.activation(
                        x0FM[:, 0:1024], xap, AF.Square, accum_out=ss[:]),
                        reads=[xbuf], writes=[x0FM, ss])
                    P.op("vector", lambda e: e.tensor_scalar_mul(
                        mu_t[:], s1[:], 1.0 / 1024.0), reads=[s1], writes=[mu_t])
                    P.op("vector", lambda e: e.tensor_scalar_mul(
                        ss[:], ss[:], 1.0 / 1024.0), reads=[ss], writes=[ss])
                    P.op("vector", lambda e: e.tensor_mul(
                        vt[:], mu_t[:], mu_t[:]), reads=[mu_t], writes=[vt])
                    P.op("vector", lambda e: e.tensor_sub(
                        rstd_t[:], ss[:], vt[:]), reads=[ss, vt], writes=[rstd_t])
                    P.op("vector", lambda e: e.tensor_scalar_add(
                        rstd_t[:], rstd_t[:], 1e-5), reads=[rstd_t], writes=[rstd_t])
                    P.op("scalar", lambda e: e.activation(
                        rstd_t[:], rstd_t[:], AF.Ln),
                        reads=[rstd_t], writes=[rstd_t])
                    P.op("scalar", lambda e: e.activation(
                        rstd_t[:], rstd_t[:], AF.Exp, scale=-0.5),
                        reads=[rstd_t], writes=[rstd_t])

                ops_ln(x0tm, xsl, stat["mu"], stat["rstd"])
                xnsl = xn0tm[:, sc * 1024:(sc + 1) * 1024]
                P.op("vector", lambda e, xsl=xsl, xnsl=xnsl, sc=sc: e.tensor_scalar(
                    xnsl, xsl, stat["mu"][sc][:], stat["rstd"][sc][:],
                    ALU.subtract, ALU.mult),
                    reads=[x0tm, stat["mu"][sc], stat["rstd"][sc]], writes=[xn0tm])
                ops_ln(xn0tm, xnsl, stat["mu2"], stat["rstd2"])
                mq = stat["mq"][sc]
                sx = stat["sx"][sc]
                P.op("vector", lambda e, xnsl=xnsl, mq=mq: e.tensor_reduce(
                    mq[:], xnsl, AX.X, ALU.max, apply_absolute_value=True),
                    reads=[xn0tm], writes=[mq])
                P.op("vector", lambda e, mq=mq: e.tensor_scalar_max(
                    mq[:], mq[:], 1e-5), reads=[mq], writes=[mq])
                P.op("vector", lambda e, c=c, sc=sc, mq=mq: e.tensor_copy(
                    mq_tm[:, c * NSC + sc: c * NSC + sc + 1], mq[:]),
                    reads=[mq], writes=[mq_tm])
                P.op("vector", lambda e, mq=mq, sx=sx: e.reciprocal(
                    sx[:], mq[:]), reads=[mq], writes=[sx])
                P.op("vector", lambda e, sx=sx: e.tensor_scalar_mul(
                    sx[:], sx[:], 127.0), reads=[sx], writes=[sx])
                for hh in range(2):
                    qsl = xqTM[:, sc * 1024 + hh * 512: sc * 1024 + (hh + 1) * 512]
                    xnh = xn0tm[:, sc * 1024 + hh * 512: sc * 1024 + (hh + 1) * 512]
                    qtmp = tfbL[hh]
                    P.op("vector", lambda e, xnh=xnh, sx=sx, qtmp=qtmp:
                         e.tensor_scalar(qtmp[:], xnh, sx[:], MAGIC,
                                         ALU.mult, ALU.add),
                         reads=[xn0tm, sx], writes=[qtmp])
                    P.op("vector", lambda e, qsl=qsl, qtmp=qtmp: e.tensor_scalar(
                        qsl, qtmp[:], MAGIC, None, ALU.subtract),
                        reads=[qtmp], writes=[xqTM])

            for fi in range(NF):
                for sc in range(NSC):
                    _transpose128(P, C, pP0[:, sc * 128:(sc + 1) * 128],
                                  x0tm[:, sc * 1024 + fi * 128:
                                       sc * 1024 + (fi + 1) * 128], pP0, x0tm)
                P.op("vector", lambda e, fi=fi: e.tensor_copy(
                    x0FM[:, fi * TC:(fi + 1) * TC], pP0[:, 0:TC]),
                    reads=[pP0], writes=[x0FM])
            for sc in range(NSC):
                xnsl = xn0tm[:, sc * 1024:(sc + 1) * 1024]
                P.op("vector", lambda e, xnsl=xnsl, sc=sc: e.tensor_scalar(
                    x0tm[:, sc * 1024:(sc + 1) * 1024], xnsl,
                    stat["mu2"][sc][:], stat["rstd2"][sc][:],
                    ALU.subtract, ALU.mult),
                    reads=[xn0tm, stat["mu2"][sc], stat["rstd2"][sc]],
                    writes=[x0tm])
            for fi in range(NF):
                for sc in range(NSC):
                    _transpose128(P, C, pP1[:, sc * 128:(sc + 1) * 128],
                                  x0tm[:, sc * 1024 + fi * 128:
                                       sc * 1024 + (fi + 1) * 128], pP1, x0tm)
                P.op("vector", lambda e, fi=fi: e.tensor_scalar(
                    xn2FM[:, fi * TC:(fi + 1) * TC], pP1[:, 0:TC],
                    lng[:, fi:fi + 1], lnb[:, fi:fi + 1], ALU.mult, ALU.add),
                    reads=[pP1, lng, lnb], writes=[xn2FM])
            for fi in range(NF):
                for sc in range(NSC):
                    def ftq(e, fi=fi, sc=sc):
                        return e.transpose(
                            pKT[:, sc * 128:(sc + 1) * 128],
                            xqTM[:, sc * 1024 + fi * 128:sc * 1024 + (fi + 1) * 128],
                            C["identb"][:])
                    P.op("tensor", ftq, reads=[xqTM, C["identb"]], writes=[pKT])
                P.op("vector", lambda e, fi=fi: e.tensor_copy(
                    xqFM[:, fi * TC:(fi + 1) * TC], pKT[:, 0:TC]),
                    reads=[pKT], writes=[xqFM])

                def fsq(e, fi=fi, c=c):
                    return e.dma_start(
                        out=xq_sp[fi * 128:(fi + 1) * 128, c * TC:(c + 1) * TC],
                        in_=xqFM[:, fi * TC:(fi + 1) * TC])
                P.dma("sync", fsq, reads=[xqFM], writes=["tok_xq_sp"])

            def tap_fm(name, src, srcbuf, fi, width=TC, dt=F32, c=c):
                tp = tap(name, (D, T), dt)
                if tp is None:
                    return

                def fd(e, fi=fi, c=c):
                    return e.dma_start(
                        out=tp[fi * 128:(fi + 1) * 128, c * TC:(c + 1) * TC],
                        in_=src)
                P.dma("sync", fd, reads=[srcbuf], writes=["taps"])

            if "xn0" in dbg:
                tpx = tap("xn0", (T, D))
                for sc in range(NSC):
                    def fd(e, sc=sc, c=c):
                        return e.dma_start(
                            out=tpx[c * TC + sc * SC: c * TC + (sc + 1) * SC, :],
                            in_=xn0tm[:, sc * 1024:(sc + 1) * 1024])
                    P.dma("sync", fd, reads=[xn0tm], writes=["taps"])

            load_wxo("Wx")
            for oc in range(NF):
                pp = pP0 if oc % 2 == 0 else pP1
                proj_oc("Wx", xn2FM, xn2FM, oc, pp)
                P.op("vector", lambda e, oc=oc, pp=pp: e.tensor_copy(
                    z1FM[:, oc * TC:(oc + 1) * TC], pp[:, 0:TC]),
                    reads=[pp], writes=[xn0tm])
            for ts_ in range(NSC):
                for half in range(2):
                    for ic in range(NF):
                        def fv(e, ts_=ts_, half=half, ic=ic):
                            return e.matmul(
                                pV[:, half * 512:(half + 1) * 512],
                                xn2FM[:, ic * TC + ts_ * 128: ic * TC + (ts_ + 1) * 128],
                                W["Wv"][:, ic * 1024 + half * 512:
                                        ic * 1024 + (half + 1) * 512],
                                start=(ic == 0), stop=(ic == NF - 1))
                        P.op("tensor", fv, reads=[xn2FM, W["Wv"]],
                             writes=[pVa if half == 0 else pVb])
                    P.op("vector", lambda e, ts_=ts_, half=half: e.tensor_copy(
                        vTM[:, ts_ * 1024 + half * 512: ts_ * 1024 + (half + 1) * 512],
                        pV[:, half * 512:(half + 1) * 512]),
                        reads=[pVa if half == 0 else pVb], writes=[vTM])
            if "v" in dbg:
                tpv = tap("v", (T, D), BF16)
                for ts_ in range(NSC):
                    def fd(e, ts_=ts_, c=c):
                        return e.dma_start(
                            out=tpv[c * TC + ts_ * SC: c * TC + (ts_ + 1) * SC, :],
                            in_=vTM[:, ts_ * 1024:(ts_ + 1) * 1024])
                    P.dma("sync", fd, reads=[vTM], writes=["taps"])

            def do_fi(fi, kfi, rfi, zfi, tfa, tfb, spfi, Dfi, eag, kTsb,
                      pjA, pjB, Asb0, Asb1):
                proj_oc("Wk", xn2FM, xn2FM, fi, pjA)
                P.op("vector", lambda e, kfi=kfi, pjA=pjA: e.tensor_copy(
                    kfi[:], pjA[:, 0:TC]), reads=[pjA], writes=[kfi])
                proj_oc("Wr", xn2FM, xn2FM, fi, pjB)
                P.op("scalar", lambda e, tfa=tfa, pjB=pjB: e.activation(
                    tfa[:], pjB[:, 0:TC], AF.Exp, scale=-1.0),
                    reads=[pjB], writes=[tfa])
                P.op("vector", lambda e, tfa=tfa: e.tensor_scalar_add(
                    tfa[:], tfa[:], 1.0), reads=[tfa], writes=[tfa])
                P.op("vector", lambda e, rfi=rfi, tfa=tfa: e.reciprocal(
                    rfi[:], tfa[:]), reads=[tfa], writes=[rfi])
                proj_oc("Ww", z1FM, xn0tm, fi, pjA)
                P.op("vector", lambda e, fi=fi, zfi=zfi, pjA=pjA: e.tensor_scalar(
                    zfi[:], pjA[:, 0:TC], bwv[:, fi:fi + 1], None, ALU.add),
                    reads=[pjA, bwv], writes=[zfi])
                P.op("scalar", lambda e, tfa=tfa, zfi=zfi: e.activation(
                    tfa[:], zfi[:], AF.Abs), reads=[zfi], writes=[tfa])
                P.op("scalar", lambda e, tfa=tfa: e.activation(
                    tfa[:], tfa[:], AF.Exp, scale=-1.0), reads=[tfa], writes=[tfa])
                P.op("scalar", lambda e, tfa=tfa: e.activation(
                    tfa[:], tfa[:], AF.Ln, bias=1.0), reads=[tfa], writes=[tfa])
                P.op("vector", lambda e, tfb=tfb, zfi=zfi: e.tensor_scalar_max(
                    tfb[:], zfi[:], 0.0), reads=[zfi], writes=[tfb])
                P.op("vector", lambda e, spfi=spfi, tfa=tfa, tfb=tfb: e.tensor_add(
                    spfi[:], tfa[:], tfb[:]), reads=[tfa, tfb], writes=[spfi])
                for sc in range(NSC):
                    sl = slice(sc * SC, (sc + 1) * SC)
                    P.op("vector", lambda e, sl=sl, Dfi=Dfi, spfi=spfi:
                         e.tensor_tensor_scan(Dfi[:, sl], spfi[:, sl],
                                              spfi[:, sl], 0.0, ALU.add,
                                              ALU.bypass),
                         reads=[spfi], writes=[Dfi])
                if "z" in dbg:
                    tap_fm("z", zfi[:], zfi, fi)
                if "sp" in dbg:
                    tap_fm("sp", spfi[:], spfi, fi)
                if "D" in dbg:
                    tap_fm("D", Dfi[:], Dfi, fi)
                if "k" in dbg:
                    tap_fm("k", kfi[:], kfi, fi)
                if "r" in dbg:
                    tap_fm("r", rfi[:], rfi, fi)

                def blk3(t, blk, n0, width):
                    return t[:, n0 * width:].rearrange(
                        "p (j k) -> p j k", k=width)[:, 0:blk, :]

                def bview(src_cols, n, width):
                    return src_cols.rearrange("p j -> p j ()").broadcast_to(
                        (128, n, width))

                def scstart(t, width):
                    return t[:].rearrange("p (s r) -> p s r", s=NSC)[:, :, 0:width]

                def earg(dst_eng, build_ops, exp_key, mul_src, mul_srcb):
                    pass

                def expmul(key, srcb, eng="vector", clamp=None, scale=1.0,
                           kfi=kfi, rfi=rfi, tfa=tfa, eag=eag):
                    P.op("scalar", lambda e: e.activation(
                        tfa[:], eag[:], AF.Exp, scale=scale),
                        reads=[eag], writes=[tfa])
                    P.op(eng, lambda e: e.tensor_mul(
                        exb[key][:], srcb[:], tfa[:]),
                        reads=[srcb, tfa], writes=[exb[key]])

                P.op("vector", lambda e, eag=eag, Dfi=Dfi: e.tensor_tensor(
                    blk3(eag, 31, 1, 16), bview(Dfi[:, 15:496:16], 31, 16),
                    blk3(Dfi, 31, 1, 16), ALU.subtract),
                    reads=[Dfi], writes=[eag])
                P.op("vector", lambda e, eag=eag, Dfi=Dfi: e.tensor_scalar_mul(
                    scstart(eag, 16), scstart(Dfi, 16), -1.0),
                    reads=[Dfi], writes=[eag])
                expmul("r16", rfi)
                P.op("vector", lambda e, eag=eag: e.tensor_scalar(
                    eag[:], eag[:], -1.0, CLAMP, ALU.mult, ALU.min),
                    reads=[eag], writes=[eag])
                expmul("kdiv", kfi)
                P.op("vector", lambda e, eag=eag, Dfi=Dfi: e.tensor_tensor(
                    blk3(eag, 32, 0, 16), blk3(Dfi, 32, 0, 16),
                    bview(Dfi[:, 15:512:16], 32, 16), ALU.subtract),
                    reads=[Dfi], writes=[eag])
                expmul("k16", kfi)
                P.op("vector", lambda e, eag=eag, Dfi=Dfi: e.tensor_tensor(
                    blk3(eag, 15, 1, 32), bview(Dfi[:, 31:480:32], 15, 32),
                    blk3(Dfi, 15, 1, 32), ALU.subtract),
                    reads=[Dfi], writes=[eag])
                P.op("vector", lambda e, eag=eag, Dfi=Dfi: e.tensor_scalar_mul(
                    scstart(eag, 32), scstart(Dfi, 32), -1.0),
                    reads=[Dfi], writes=[eag])
                expmul("r32", rfi)
                P.op("vector", lambda e, eag=eag, Dfi=Dfi: e.tensor_tensor(
                    blk3(eag, 16, 0, 32), blk3(Dfi, 16, 0, 32),
                    bview(Dfi[:, 31:512:32], 16, 32), ALU.subtract),
                    reads=[Dfi], writes=[eag])
                expmul("k32", kfi)
                P.op("vector", lambda e, eag=eag, Dfi=Dfi: e.tensor_tensor(
                    blk3(eag, 7, 1, 64), bview(Dfi[:, 63:448:64], 7, 64),
                    blk3(Dfi, 7, 1, 64), ALU.subtract),
                    reads=[Dfi], writes=[eag])
                P.op("vector", lambda e, eag=eag, Dfi=Dfi: e.tensor_scalar_mul(
                    scstart(eag, 64), scstart(Dfi, 64), -1.0),
                    reads=[Dfi], writes=[eag])
                expmul("r64", rfi)
                P.op("vector", lambda e, eag=eag, Dfi=Dfi: e.tensor_tensor(
                    blk3(eag, 8, 0, 64), blk3(Dfi, 8, 0, 64),
                    bview(Dfi[:, 63:512:64], 8, 64), ALU.subtract),
                    reads=[Dfi], writes=[eag])
                expmul("k64", kfi)
                P.op("scalar", lambda e, tfa=tfa, Dfi=Dfi: e.activation(
                    tfa[:], Dfi[:], AF.Exp, scale=-1.0),
                    reads=[Dfi], writes=[tfa])
                P.op("vector", lambda e, rfi=rfi, tfa=tfa: e.tensor_mul(
                    exb["rhat"][:], rfi[:], tfa[:]),
                    reads=[rfi, tfa], writes=[exb["rhat"]])
                P.op("vector", lambda e, eag=eag, Dfi=Dfi: e.tensor_tensor(
                    blk3(eag, 4, 0, 128), blk3(Dfi, 4, 0, 128),
                    bview(Dfi[:, 127:512:128], 4, 128), ALU.subtract),
                    reads=[Dfi], writes=[eag])
                expmul("khat", kfi)
                P.op("scalar", lambda e, Dfi=Dfi: e.activation(
                    dec4[:], Dfi[:, 127:512:128], AF.Exp, scale=-1.0),
                    reads=[Dfi], writes=[dec4])

                for sc in range(NSC):
                    sl = slice(sc * SC, (sc + 1) * SC)
                    pA = (pA0, pA1)
                    AsbP = (Asb0, Asb1)
                    for q in (0, 1):
                        pr = slice(64 * q, 64 * q + 64)
                        def f_in(e, q=q, sl=sl, pr=pr):
                            return e.matmul(pA[q][:, 0:128],
                                            exb["kdiv"][pr, sl],
                                            exb["r16"][pr, sl],
                                            start=True, stop=True,
                                            tile_position=(64 * q, 0))
                        P.op("tensor", f_in,
                             reads=[exb["kdiv"], exb["r16"]], writes=[pA[q]])
                        for m in range(4):
                            def f_p16(e, q=q, m=m, sc=sc, pr=pr):
                                s0 = 32 * m
                                t0 = 32 * m + 16
                                return e.matmul(
                                    pA[q][s0:s0 + 16, t0:t0 + 16],
                                    exb["k16"][pr, sc * SC + s0: sc * SC + s0 + 16],
                                    exb["r16"][pr, sc * SC + t0: sc * SC + t0 + 16],
                                    start=True, stop=True,
                                    tile_position=(64 * q, s0))
                            P.op("tensor", f_p16,
                                 reads=[exb["k16"], exb["r16"]], writes=[pA[q]])
                        for m in range(2):
                            def f_p32(e, q=q, m=m, sc=sc, pr=pr):
                                s0 = 64 * m
                                t0 = 64 * m + 32
                                return e.matmul(
                                    pA[q][s0:s0 + 32, t0:t0 + 32],
                                    exb["k32"][pr, sc * SC + s0: sc * SC + s0 + 32],
                                    exb["r32"][pr, sc * SC + t0: sc * SC + t0 + 32],
                                    start=True, stop=True,
                                    tile_position=(64 * q, s0))
                            P.op("tensor", f_p32,
                                 reads=[exb["k32"], exb["r32"]], writes=[pA[q]])
                        def f_p64(e, q=q, sc=sc, pr=pr):
                            return e.matmul(
                                pA[q][0:64, 64:128],
                                exb["k64"][pr, sc * SC: sc * SC + 64],
                                exb["r64"][pr, sc * SC + 64: sc * SC + 128],
                                start=True, stop=True,
                                tile_position=(64 * q, 0))
                        P.op("tensor", f_p64,
                             reads=[exb["k64"], exb["r64"]], writes=[pA[q]])
                        P.op("vector", lambda e, q=q, AsbP=AsbP: e.tensor_tensor(
                            AsbP[q][:], pA[q][:, 0:128], C["tril"][:], ALU.mult),
                            reads=[pA[q], C["tril"]], writes=[AsbP[q]])
                    def f_kt(e, sl=sl):
                        return e.transpose(pKT[:, 0:128], exb["khat"][:, sl],
                                           C["identb"][:])
                    P.op("tensor", f_kt, reads=[exb["khat"], C["identb"]],
                         writes=[pKT])
                    P.op("vector", lambda e, kTsb=kTsb: e.tensor_copy(
                        kTsb[:], pKT[:, 0:128]), reads=[pKT], writes=[kTsb])
                    for q in (0, 1):
                        pr = slice(64 * q, 64 * q + 64)
                        h = 2 * fi + q
                        vsl = vTM[:, sc * 1024 + h * 64: sc * 1024 + (h + 1) * 64]
                        def f_av(e, q=q, vsl=vsl, AsbP=AsbP):
                            return e.matmul(pOut[64 * q:64 * q + 64, 0:128],
                                            vsl, AsbP[q][:],
                                            start=True, stop=False,
                                            tile_position=(0, 64 * q))
                        P.op("tensor", f_av, reads=[vTM, AsbP[q]], writes=[pOut])
                        def f_int(e, q=q, fi=fi, sl=sl, pr=pr):
                            return e.matmul(pOut[64 * q:64 * q + 64, 0:128],
                                            sStB[fi][pr, 0:64],
                                            exb["rhat"][pr, sl],
                                            start=False, stop=True,
                                            tile_position=(64 * q, 64 * q))
                        P.op("tensor", f_int, reads=[sStB[fi], exb["rhat"]],
                             writes=[pOut])
                        def f_su(e, q=q, vsl=vsl, kTsb=kTsb):
                            return e.matmul(pA0[64 * q:64 * q + 64, 0:64],
                                            kTsb[:, 64 * q:64 * q + 64], vsl,
                                            start=True, stop=True,
                                            tile_position=(0, 64 * q))
                        P.op("tensor", f_su, reads=[kTsb, vTM], writes=[pA0])
                    cur, oth = state_cur[fi], state_oth[fi]
                    P.op("vector", lambda e, cur=cur, oth=oth, sc=sc:
                         e.scalar_tensor_tensor(
                             oth[:], cur[:], dec4[:, sc:sc + 1],
                             pA0[:, 0:64], ALU.mult, ALU.add),
                         reads=[cur, dec4, pA0], writes=[oth])
                    state_cur[fi], state_oth[fi] = oth, cur
                    P.op("vector", lambda e, fi=fi, oth=oth: e.tensor_copy(
                        sStB[fi][:], oth[:]), reads=[oth], writes=[sStB[fi]])
                    P.op("vector", lambda e, fi=fi, sl=sl: e.tensor_copy(
                        outFM[:, fi * TC:][:, sl], pOut[:, 0:128]),
                        reads=[pOut], writes=[outFM])
                if "outpre" in dbg:
                    tap_fm("outpre", outFM[:, fi * TC:(fi + 1) * TC], outFM,
                           fi, dt=BF16)

            for fi in range(NF):
                par = fi % 2
                pjA = pP0 if par == 0 else pVa
                pjB = pP1 if par == 0 else pVb
                do_fi(fi, kfiL[par], rfiL[par], zfiL[par], tfaL[par],
                      tfbL[par], spfiL[par], DfiL[par], eagL[par],
                      kTsbL[par], pjA, pjB, Asb[2 * par], Asb[2 * par + 1])

            load_wxo("Wo")
            for oc in range(NF):
                pp = pP0 if oc % 2 == 0 else pP1
                proj_oc("Wo", outFM, outFM, oc, pp)
                P.op("vector", lambda e, oc=oc, pp=pp: e.tensor_tensor(
                    x1pFM[:, oc * TC:(oc + 1) * TC], pp[:, 0:TC],
                    x0FM[:, oc * TC:(oc + 1) * TC], ALU.add),
                    reads=[pp, x0FM], writes=[x1pFM])
            def f_x1p(e, c=c):
                dst = x1p_sp.rearrange("(a p) t -> p a t", p=128)
                return e.dma_start(out=dst[:, :, c * TC:(c + 1) * TC],
                                   in_=x1pFM[:].rearrange("p (a t) -> p a t", a=NF))
            P.dma("sync", f_x1p, reads=[x1pFM], writes=["tok_x1p_sp"])
            if "x1p" in dbg:
                tpp = tap("x1p", (D, T))
                def f_t(e, c=c):
                    dst = tpp.rearrange("(a p) t -> p a t", p=128)
                    return e.dma_start(out=dst[:, :, c * TC:(c + 1) * TC],
                                       in_=x1pFM[:].rearrange("p (a t) -> p a t", a=NF))
                P.dma("sync", f_t, reads=[x1pFM], writes=["taps"])

        P.phase_barrier()
        stA.close()
        stB = st.enter_context(ExitStack())
        xqB = sb(nc, stB, "xqB", (128, NF * TC), BF16)
        qgBL = [sb(nc, stB, f"qgB{i}", (128, NF * 512), BF16) for i in range(2)]
        quBL = [sb(nc, stB, f"quB{i}", (128, NF * 512), BF16) for i in range(2)]
        hFM = sb(nc, stB, "hFM", (128, 32 * 512))
        hqB = sb(nc, stB, "hqB", (128, 32 * 512), BF16)
        x1pB = sb(nc, stB, "x1pB", (128, NF * TC))
        x1B = sb(nc, stB, "x1B", (128, NF * TC))
        xn1B = sb(nc, stB, "xn1B", (128, NF * TC))
        qdBL = [qgBL[0], qgBL[1]]
        hbuf = sb(nc, stB, "hbuf", (128, 512))
        hbuf2 = sb(nc, stB, "hbuf2", (128, 512))
        dqb_sb = sb(nc, stB, "dqb_sb", (128, 512))
        qb_sb = sb(nc, stB, "qb_sb", (128, 512))
        rowB = sb(nc, stB, "rowB", (1, 1024))
        rmH = sb(nc, stB, "rmH", (128, NSC))
        rmH2 = sb(nc, stB, "rmH2", (128, NSC))
        rtmp = sb(nc, stB, "rtmp", (128, NSC))
        fh = sb(nc, stB, "fh", (128, NSC))
        sh = sb(nc, stB, "sh", (128, NSC))
        dqt = sb(nc, stB, "dqt", (128, NSC))
        srow = sb(nc, stB, "srow", (1, 1024))
        srow2 = sb(nc, stB, "srow2", (1, 512))
        srow3 = sb(nc, stB, "srow3", (1, 512))

        for c in range(nch):
            for a_ in range(NF):
                def flq(e, c=c, a_=a_):
                    return e.dma_start(
                        out=xqB[:, a_ * TC:(a_ + 1) * TC],
                        in_=xq_sp[a_ * 128:(a_ + 1) * 128, c * TC:(c + 1) * TC])
                P.dma("sync", flq, reads=["tok_xq_sp"], writes=[xqB])

            for a_ in range(NF):
                def flx(e, c=c, a_=a_):
                    return e.dma_start(
                        out=x1pB[:, a_ * TC:(a_ + 1) * TC],
                        in_=x1p_sp[a_ * 128:(a_ + 1) * 128, c * TC:(c + 1) * TC])
                P.dma("sync", flx, reads=["tok_x1p_sp"], writes=[x1pB])

            def load_gu(ic):
                qgB = qgBL[ic % 2]
                quB = quBL[ic % 2]
                for a_ in range(NF):
                    def flg(e, ic=ic, a_=a_, qgB=qgB):
                        return e.dma_start(
                            out=qgB[:, a_ * 512:(a_ + 1) * 512],
                            in_=qgT_sp[a_ * 128:(a_ + 1) * 128,
                                       ic * 512:(ic + 1) * 512])
                    P.dma("sync", flg, writes=[qgB])
                    def flu(e, ic=ic, a_=a_, quB=quB):
                        return e.dma_start(
                            out=quB[:, a_ * 512:(a_ + 1) * 512],
                            in_=quT_sp[a_ * 128:(a_ + 1) * 128,
                                       ic * 512:(ic + 1) * 512])
                    P.dma("sync", flu, writes=[quB])
            load_gu(0)
            for ic in range(8):
                qgB = qgBL[ic % 2]
                quB = quBL[ic % 2]
                if ic + 1 < 8:
                    load_gu(ic + 1)
                for ib in range(4):
                    for fc in range(NF):
                        def fg(e, ic=ic, ib=ib, fc=fc, qgB=qgB):
                            return e.matmul(
                                pP0[:, 0:512],
                                qgB[:, fc * 512 + ib * 128: fc * 512 + (ib + 1) * 128],
                                xqB[:, fc * TC:(fc + 1) * TC],
                                start=(fc == 0), stop=(fc == NF - 1))
                        P.op("tensor", fg, reads=[qgB, xqB], writes=[pP0])
                    for fc in range(NF):
                        def fu(e, ic=ic, ib=ib, fc=fc, quB=quB):
                            return e.matmul(
                                pP1[:, 0:512],
                                quB[:, fc * 512 + ib * 128: fc * 512 + (ib + 1) * 128],
                                xqB[:, fc * TC:(fc + 1) * TC],
                                start=(fc == 0), stop=(fc == NF - 1))
                        P.op("tensor", fu, reads=[quB, xqB], writes=[pP1])
                    it = ic * 4 + ib
                    P.op("vector", lambda e: e.tensor_scalar_max(
                        hbuf[:], pP0[:, 0:512], 0.0),
                        reads=[pP0], writes=[hbuf])
                    P.op("scalar", lambda e: e.activation(
                        hbuf[:], hbuf[:], AF.Square),
                        reads=[hbuf], writes=[hbuf])
                    P.op("vector", lambda e, it=it: e.tensor_tensor(
                        hFM[:, it * 512:(it + 1) * 512], hbuf[:], pP1[:, 0:512],
                        ALU.mult),
                        reads=[hbuf, pP1], writes=[hFM])

            P.op("vector", lambda e: e.memset(rmH[:], 0.0), writes=[rmH])
            rms_ = [rmH, rmH2]
            for it in range(32):
                for tsb in range(NSC):
                    _transpose128(P, C, pP0[:, tsb * 128:(tsb + 1) * 128],
                                  hFM[:, it * 512 + tsb * 128:
                                      it * 512 + (tsb + 1) * 128], pP0, hFM)
                P.op("vector", lambda e: e.tensor_reduce(
                    rtmp[:], pP0[:, 0:512].rearrange("p (j k) -> p j k", k=128),
                    AX.X, ALU.max, apply_absolute_value=True),
                    reads=[pP0], writes=[rtmp])
                a0, a1 = rms_[it % 2], rms_[(it + 1) % 2]
                P.op("vector", lambda e, a0=a0, a1=a1: e.tensor_max(
                    a1[:], a0[:], rtmp[:]),
                    reads=[a0, rtmp], writes=[a1])

            P.op("vector", lambda e, c=c: e.tensor_scalar_mul(
                fh[:], mq_tm[:, c * NSC:(c + 1) * NSC], 1.0 / 127.0),
                reads=[mq_tm], writes=[fh])
            P.op("vector", lambda e: e.tensor_mul(sh[:], fh[:], fh[:]),
                 reads=[fh], writes=[sh])
            P.op("vector", lambda e: e.tensor_mul(dqt[:], sh[:], fh[:]),
                 reads=[fh, sh], writes=[dqt])
            P.op("vector", lambda e: e.tensor_scalar_mul(dqt[:], dqt[:],
                                                         cfB[:, 0:1]),
                 reads=[dqt, cfB], writes=[dqt])
            P.op("vector", lambda e: e.tensor_tensor(sh[:], rmH[:], dqt[:],
                                                     ALU.mult),
                 reads=[rmH, dqt], writes=[sh])
            P.op("vector", lambda e: e.tensor_scalar_max(sh[:], sh[:], 1e-5),
                 reads=[sh], writes=[sh])
            P.op("vector", lambda e: e.reciprocal(fh[:], sh[:]),
                 reads=[sh], writes=[fh])
            P.op("vector", lambda e: e.tensor_scalar_mul(fh[:], fh[:], 127.0),
                 reads=[fh], writes=[fh])
            P.op("vector", lambda e: e.tensor_mul(rtmp[:], dqt[:], fh[:]),
                 reads=[dqt, fh], writes=[rtmp])
            P.op("vector", lambda e: e.tensor_scalar_mul(fh[:], sh[:],
                                                         dfB[:, 0:1]),
                 reads=[sh, dfB], writes=[fh])
            for tsb in range(NSC):
                def ftr(e, tsb=tsb):
                    return e.transpose(pV[0:1, tsb * 128:(tsb + 1) * 128],
                                       rtmp[:, tsb:tsb + 1], C["ident"][:])
                P.op("tensor", ftr, reads=[rtmp, C["ident"]], writes=[pV])
                def ftr2(e, tsb=tsb):
                    return e.transpose(pV[0:1, 512 + tsb * 128: 512 + (tsb + 1) * 128],
                                       fh[:, tsb:tsb + 1], C["ident"][:])
                P.op("tensor", ftr2, reads=[fh, C["ident"]], writes=[pV])
            P.op("vector", lambda e: e.tensor_copy(rowB[0:1, :], pV[0:1, 0:1024]),
                 reads=[pV], writes=[rowB])
            def fbq(e):
                return e.matmul(pP0[:, 0:512], C["ones1"][:], rowB[0:1, 0:512],
                                start=True, stop=True)
            P.op("tensor", fbq, reads=[C["ones1"], rowB], writes=[pP0])
            P.op("vector", lambda e: e.tensor_copy(qb_sb[:], pP0[:, 0:512]),
                 reads=[pP0], writes=[qb_sb])
            def fbd(e):
                return e.matmul(pP1[:, 0:512], C["ones1"][:], rowB[0:1, 512:1024],
                                start=True, stop=True)
            P.op("tensor", fbd, reads=[C["ones1"], rowB], writes=[pP1])
            P.op("vector", lambda e: e.tensor_copy(dqb_sb[:], pP1[:, 0:512]),
                 reads=[pP1], writes=[dqb_sb])

            for it in range(32):
                P.op("vector", lambda e, it=it: e.tensor_tensor(
                    hbuf2[:], hFM[:, it * 512:(it + 1) * 512], qb_sb[:],
                    ALU.mult),
                    reads=[hFM, qb_sb], writes=[hbuf2])
                P.op("vector", lambda e: e.tensor_scalar_add(
                    hbuf2[:], hbuf2[:], MAGIC),
                    reads=[hbuf2], writes=[hbuf2])
                P.op("vector", lambda e, it=it: e.tensor_scalar(
                    hqB[:, it * 512:(it + 1) * 512], hbuf2[:], MAGIC, None,
                    ALU.subtract),
                    reads=[hbuf2], writes=[hqB])

            def load_qd(oc):
                qdB = qdBL[oc % 2]
                for a_ in range(32):
                    def fld(e, oc=oc, a_=a_, qdB=qdB):
                        return e.dma_start(
                            out=qdB[:, a_ * 128:(a_ + 1) * 128],
                            in_=qdT_sp[a_ * 128:(a_ + 1) * 128,
                                       oc * 128:(oc + 1) * 128])
                    P.dma("sync", fld, writes=[qdB])
            load_qd(0)
            for oc in range(NF):
                qdB = qdBL[oc % 2]
                if oc + 1 < NF:
                    load_qd(oc + 1)
                pp = pP0 if oc % 2 == 0 else pP1
                for it in range(32):
                    def fd(e, oc=oc, it=it, pp=pp, qdB=qdB):
                        return e.matmul(pp[:, 0:512],
                                        qdB[:, it * 128:(it + 1) * 128],
                                        hqB[:, it * 512:(it + 1) * 512],
                                        start=(it == 0), stop=(it == 31))
                    P.op("tensor", fd, reads=[qdB, hqB], writes=[pp])
                P.op("vector", lambda e, pp=pp: e.tensor_tensor(
                    hbuf[:], pp[:, 0:512], dqb_sb[:], ALU.mult),
                    reads=[pp, dqb_sb], writes=[hbuf])
                P.op("vector", lambda e, oc=oc: e.tensor_tensor(
                    x1B[:, oc * TC:(oc + 1) * TC], hbuf[:],
                    x1pB[:, oc * TC:(oc + 1) * TC], ALU.add),
                    reads=[hbuf, x1pB], writes=[x1B])

            for oc in range(NF):
                P.op("scalar", lambda e, oc=oc: e.activation(
                    hbuf[:], x1B[:, oc * TC:(oc + 1) * TC], AF.Square),
                    reads=[x1B], writes=[hbuf])
                def fsq2(e, oc=oc):
                    return e.matmul(pV[0:1, 0:512], C["onesk"][:], hbuf[:],
                                    start=(oc == 0), stop=(oc == NF - 1))
                P.op("tensor", fsq2, reads=[C["onesk"], hbuf], writes=[pV])
                def fsm(e, oc=oc):
                    return e.matmul(pV[0:1, 512:1024], C["onesk"][:],
                                    x1B[:, oc * TC:(oc + 1) * TC],
                                    start=(oc == 0), stop=(oc == NF - 1))
                P.op("tensor", fsm, reads=[C["onesk"], x1B], writes=[pV])
            P.op("vector", lambda e: e.tensor_scalar_mul(
                srow[0:1, 0:512], pV[0:1, 512:1024], 1.0 / 1024.0),
                reads=[pV], writes=[srow])
            P.op("vector", lambda e: e.tensor_scalar_mul(
                srow[0:1, 512:1024], pV[0:1, 0:512], 1.0 / 1024.0),
                reads=[pV], writes=[srow])
            P.op("vector", lambda e: e.tensor_mul(
                srow3[0:1, :], srow[0:1, 0:512], srow[0:1, 0:512]),
                reads=[srow], writes=[srow3])
            P.op("vector", lambda e: e.tensor_sub(
                srow2[0:1, :], srow[0:1, 512:1024], srow3[0:1, :]),
                reads=[srow, srow3], writes=[srow2])
            P.op("vector", lambda e: e.tensor_scalar_add(
                srow2[0:1, :], srow2[0:1, :], 1e-5),
                reads=[srow2], writes=[srow2])
            P.op("scalar", lambda e: e.activation(srow2[0:1, :], srow2[0:1, :],
                                                  AF.Ln),
                 reads=[srow2], writes=[srow2])
            P.op("scalar", lambda e: e.activation(srow2[0:1, :], srow2[0:1, :],
                                                  AF.Exp, scale=-0.5),
                 reads=[srow2], writes=[srow2])
            def fbm(e):
                return e.matmul(pP0[:, 0:512], C["ones1"][:], srow[0:1, 0:512],
                                start=True, stop=True)
            P.op("tensor", fbm, reads=[C["ones1"], srow], writes=[pP0])
            def fbr(e):
                return e.matmul(pP1[:, 0:512], C["ones1"][:], srow2[0:1, :],
                                start=True, stop=True)
            P.op("tensor", fbr, reads=[C["ones1"], srow2], writes=[pP1])
            for oc in range(NF):
                P.op("vector", lambda e, oc=oc: e.tensor_sub(
                    hbuf[:], x1B[:, oc * TC:(oc + 1) * TC], pP0[:, 0:512]),
                    reads=[x1B, pP0], writes=[hbuf])
                P.op("vector", lambda e, oc=oc: e.tensor_tensor(
                    xn1B[:, oc * TC:(oc + 1) * TC], hbuf[:], pP1[:, 0:512],
                    ALU.mult),
                    reads=[hbuf, pP1], writes=[xn1B])
                P.op("vector", lambda e, oc=oc: e.tensor_reduce(
                    rtmp[:, 0:1], xn1B[:, oc * TC:(oc + 1) * TC], AX.X,
                    ALU.max, apply_absolute_value=True),
                    reads=[xn1B], writes=[rtmp])
                a0, a1 = pmx_pp[pmx_n[0] % 2], pmx_pp[(pmx_n[0] + 1) % 2]
                pmx_n[0] += 1
                P.op("vector", lambda e, a0=a0, a1=a1: e.tensor_max(
                    a1[:], a0[:], rtmp[:, 0:1]),
                    reads=[a0, rtmp], writes=[a1])
            def fo1(e, c=c):
                dst = out["x1"].rearrange("(a p) t -> p a t", p=128)
                return e.dma_start(out=dst[:, :, c * TC:(c + 1) * TC],
                                   in_=x1B[:].rearrange("p (a t) -> p a t", a=NF))
            P.dma("sync", fo1, reads=[x1B], writes=["out_x1"])
            def fo2(e, c=c):
                dst = out["xn1"].rearrange("(a p) t -> p a t", p=128)
                return e.dma_start(out=dst[:, :, c * TC:(c + 1) * TC],
                                   in_=xn1B[:].rearrange("p (a t) -> p a t", a=NF))
            P.dma("sync", fo2, reads=[xn1B], writes=["out_xn1"])
            if "hq" in dbg:
                tph = tap("hq", (I0, T), BF16)
                def fth(e, c=c):
                    dst = tph.rearrange("(a p) t -> p a t", p=128)
                    return e.dma_start(out=dst[:, :, c * TC:(c + 1) * TC],
                                       in_=hqB[:].rearrange("p (a t) -> p a t",
                                                            a=32))
                P.dma("sync", fth, reads=[hqB], writes=["taps"])

        pmx_fin = pmx_pp[pmx_n[0] % 2]
        _transpose128(P, C, pP0[0:1, 0:128], pmx_fin[:, 0:1], pP0, pmx_fin)
        P.op("vector", lambda e: e.tensor_reduce(
            pmxr[:], pP0[0:1, 0:128], AX.X, ALU.max), reads=[pP0], writes=[pmxr])
        P.dma("sync", lambda e: e.dma_start(out=out["pmx"], in_=pmxr[:]),
              reads=[pmxr], writes=["out_pmx"])

        P.barrier_all(["out_x1", "out_xn1", "out_pmx", "taps", "tok_xq_sp"])

        with nc.Block() as block:
            P.emit(block, st)
    return nc, dbg_out



def _sigmoid_ops(P, out_ap, out_buf, in_ap, in_buf, tmp_ap, tmp_buf):
    P.op("scalar", lambda e: e.activation(tmp_ap, in_ap, AF.Exp, scale=-1.0),
         reads=[in_buf], writes=[tmp_buf])
    P.op("vector", lambda e: e.tensor_scalar_add(tmp_ap, tmp_ap, 1.0),
         reads=[tmp_buf], writes=[tmp_buf])
    P.op("vector", lambda e: e.reciprocal(out_ap, tmp_ap),
         reads=[tmp_buf], writes=[out_buf])



def build_k2(nch=NCH, dbg=()):
    nc = bass.Bass("TRN2", target_bir_lowering=False)
    inp = {}
    for name, shape in [
        ("xn1", [D, T]), ("pmx8", [8, 1]),
        ("W_lambda", [16, D]), ("b_lambda", [16]), ("W_B", [16, D]),
        ("W_C", [D, 16]), ("W_gate", [D, D]), ("rms_weight", [D]),
        ("wg1", [I1, D]), ("wu1", [I1, D]),
    ]:
        inp[name] = nc.dram_tensor(name, shape, F32, kind="ExternalInput").ap()
    out = {
        "attn1": nc.dram_tensor("attn1", [D, T], F32, kind="ExternalOutput").ap(),
        "gate_sp": nc.dram_tensor("gate_sp", [I1, T], F32, kind="ExternalOutput").ap(),
        "up_sp": nc.dram_tensor("up_sp", [I1, T], F32, kind="ExternalOutput").ap(),
        "pmg": nc.dram_tensor("pmg", [1, 1], F32, kind="ExternalOutput").ap(),
        "pmu": nc.dram_tensor("pmu", [1, 1], F32, kind="ExternalOutput").ap(),
    }
    dbg_out = {}

    def tap(name, shape, dtype=F32):
        if name in dbg and name not in dbg_out:
            dbg_out[name] = nc.dram_tensor("tap_" + name, list(shape), dtype,
                                           kind="ExternalOutput").ap()
        return dbg_out.get(name)

    P = Prog(nc)
    NOC1 = I1 // 128

    with ExitStack() as st:
        C = _setup_consts(nc, P, st)
        rmsw = _load_fvec(nc, P, st, "rmsw", inp["rms_weight"])
        pP0 = ps(nc, st, "pP0", (128, 512))
        pP1 = ps(nc, st, "pP1", (128, 512))
        pP2 = ps(nc, st, "pP2", (128, 512))
        pSt = ps(nc, st, "pSt", (128, 1024))
        pKT = ps(nc, st, "pKT", (128, 1024), BF16)

        WgT = sb(nc, st, "WgT", (128, NF * 1024), BF16)
        WlamT = sb(nc, st, "WlamT", (128, NF * 16))
        WBT = sb(nc, st, "WBT", (128, NF * 16))
        WCT = sb(nc, st, "WCT", (16, 1024))
        blam = sb(nc, st, "blam", (16, 1))
        sxB = sb(nc, st, "sxB", (128, 1))
        ixB = sb(nc, st, "ixB", (128, 1))
        scr = sb(nc, st, "scr", (1, 8))
        pmg128 = sb(nc, st, "pmg128", (128, 1))
        pmg128b = sb(nc, st, "pmg128b", (128, 1))
        pmu128 = sb(nc, st, "pmu128", (128, 1))
        pmu128b = sb(nc, st, "pmu128b", (128, 1))
        ulam = sb(nc, st, "ulam", (16, T))
        Bxs = sb(nc, st, "Bxs", (16, T))
        t16 = sb(nc, st, "t16", (16, T))
        lam = ulam
        hsS = t16
        prow = sb(nc, st, "prow", (1, 512))

        stGU = st.enter_context(ExitStack())
        wgT = sb(nc, stGU, "wgT", (128, NF * I1), BF16)
        wuT = sb(nc, stGU, "wuT", (128, NF * I1), BF16)
        with ExitStack() as stw:
            wt = sb(nc, stw, "wt", (128, 1024))
            wq = sb(nc, stw, "wq", (128, 1024))
            bs = sb(nc, stw, "bs", (128, 32))
            br = sb(nc, stw, "br", (128, 32))
            qb = sb(nc, stw, "qb", (128, 1024), BF16)

            def bview32(t):
                return t[:].rearrange("p b -> p b ()").broadcast_to((128, 32, 32))

            wt2 = sb(nc, stw, "wt2", (128, 1024))
            qb2 = sb(nc, stw, "qb2", (128, 1024), BF16)

            def lpbq(inp_ap, rows, dst):
                for it in range(rows // 128):
                    wtp = wt if it % 2 == 0 else wt2
                    qbp = qb if it % 2 == 0 else qb2
                    def fl(e, it=it, inp_ap=inp_ap, wtp=wtp):
                        return e.dma_start(out=wtp[:],
                                           in_=inp_ap[it * 128:(it + 1) * 128, :])
                    P.dma("sync", fl, writes=[wtp])
                    P.op("vector", lambda e, wtp=wtp: e.tensor_reduce(
                        bs[:], wtp[:].rearrange("p (b k) -> p b k", k=32), AX.X,
                        ALU.max, apply_absolute_value=True),
                        reads=[wtp], writes=[bs])
                    P.op("vector", lambda e: e.tensor_scalar(
                        bs[:], bs[:], 1.0 / 7.0, 1e-12, ALU.mult, ALU.max),
                        reads=[bs], writes=[bs])
                    P.op("vector", lambda e: e.reciprocal(br[:], bs[:]),
                         reads=[bs], writes=[br])
                    P.op("vector", lambda e, wtp=wtp: e.tensor_tensor(
                        wq[:].rearrange("p (b k) -> p b k", k=32),
                        wtp[:].rearrange("p (b k) -> p b k", k=32),
                        bview32(br), ALU.mult), reads=[wtp, br], writes=[wq])
                    P.op("vector", lambda e: e.tensor_scalar_add(
                        wq[:], wq[:], MAGIC), reads=[wq], writes=[wq])
                    P.op("vector", lambda e: e.tensor_scalar(
                        wq[:], wq[:], MAGIC, 7.0, ALU.subtract, ALU.min),
                        reads=[wq], writes=[wq])
                    P.op("vector", lambda e: e.tensor_scalar_max(
                        wq[:], wq[:], -8.0), reads=[wq], writes=[wq])
                    P.op("vector", lambda e, qbp=qbp: e.tensor_tensor(
                        qbp[:].rearrange("p (b k) -> p b k", k=32),
                        wq[:].rearrange("p (b k) -> p b k", k=32),
                        bview32(bs), ALU.mult), reads=[wq, bs], writes=[qbp])
                    for fc in range(8):
                        def ft(e, fc=fc, qbp=qbp):
                            return e.transpose(pKT[:, fc * 128:(fc + 1) * 128],
                                               qbp[:, fc * 128:(fc + 1) * 128],
                                               C["identb"][:])
                        P.op("tensor", ft, reads=[qbp, C["identb"]], writes=[pKT])
                    def fe(e, it=it, dst=dst):
                        return e.tensor_copy(
                            dst[:].rearrange("p (f i) -> p f i", i=I1)
                                  [:, :, it * 128:(it + 1) * 128],
                            pKT[:, 0:1024].rearrange("p (f j) -> p f j", f=8))
                    P.op("vector", fe, reads=[pKT], writes=[dst])

            lpbq(inp["wg1"], I1, wgT)
            lpbq(inp["wu1"], I1, wuT)
            tpw = tap("wgt", (128, NF * I1), BF16)
            if tpw is not None:
                P.dma("sync", lambda e: e.dma_start(out=tpw, in_=wgT[:]),
                      reads=[wgT], writes=["taps"])
            tpu = tap("wut", (128, NF * I1), BF16)
            if tpu is not None:
                P.dma("sync", lambda e: e.dma_start(out=tpu, in_=wuT[:]),
                      reads=[wuT], writes=["taps"])
            for it in range(8):
                def fl(e, it=it):
                    return e.dma_start(out=wt[:],
                                       in_=inp["W_gate"][it * 128:(it + 1) * 128, :])
                P.dma("sync", fl, writes=[wt])
                P.op("vector", lambda e: e.tensor_copy(qb[:], wt[:]),
                     reads=[wt], writes=[qb])
                for fc in range(8):
                    def ft(e, fc=fc):
                        return e.transpose(pKT[:, fc * 128:(fc + 1) * 128],
                                           qb[:, fc * 128:(fc + 1) * 128],
                                           C["identb"][:])
                    P.op("tensor", ft, reads=[qb, C["identb"]], writes=[pKT])
                for fc in range(8):
                    def fe(e, fc=fc, it=it):
                        return e.tensor_copy(
                            WgT[:, fc * 1024 + it * 128: fc * 1024 + (it + 1) * 128],
                            pKT[:, fc * 128:(fc + 1) * 128])
                    P.op("vector", fe, reads=[pKT], writes=[WgT])
            for wname, dst in (("W_lambda", WlamT), ("W_B", WBT)):
                def fl(e, wname=wname):
                    return e.dma_start(out=wt[0:16, :], in_=inp[wname][:])
                P.dma("sync", fl, writes=[wt])
                for fc in range(8):
                    def ft(e, fc=fc):
                        return e.transpose(pP0[:, 0:16],
                                           wt[0:16, fc * 128:(fc + 1) * 128],
                                           C["ident"][0:16, 0:16])
                    P.op("tensor", ft, reads=[wt, C["ident"]], writes=[pP0])
                    def fe(e, fc=fc, dst=dst):
                        return e.tensor_copy(dst[:, fc * 16:(fc + 1) * 16],
                                             pP0[:, 0:16])
                    P.op("vector", fe, reads=[pP0], writes=[dst])
            for it in range(8):
                def fl(e, it=it):
                    return e.dma_start(out=wt[:, 0:16],
                                       in_=inp["W_C"][it * 128:(it + 1) * 128, :])
                P.dma("sync", fl, writes=[wt])
                def ft(e, it=it):
                    return e.transpose(pP0[0:16, 0:128], wt[:, 0:16], C["ident"][:])
                P.op("tensor", ft, reads=[wt, C["ident"]], writes=[pP0])
                def fe(e, it=it):
                    return e.tensor_copy(WCT[0:16, it * 128:(it + 1) * 128],
                                         pP0[0:16, 0:128])
                P.op("vector", fe, reads=[pP0], writes=[WCT])
            def flb(e):
                return e.dma_start(out=blam[:], in_=inp["b_lambda"].rearrange(
                    "(n o) -> n o", o=1))
            P.dma("sync", flb, writes=[blam])
            def flp(e):
                return e.dma_start(out=wt[0:8, 0:1], in_=inp["pmx8"][:])
            P.dma("sync", flp, writes=[wt])
            def ftp(e):
                return e.transpose(pP0[0:1, 0:8], wt[0:8, 0:1], C["ident"][0:8, 0:8])
            P.op("tensor", ftp, reads=[wt, C["ident"]], writes=[pP0])
            P.op("vector", lambda e: e.tensor_reduce(
                scr[0:1, 0:1], pP0[0:1, 0:8], AX.X, ALU.max),
                reads=[pP0], writes=[scr])
            P.op("vector", lambda e: e.tensor_scalar(
                scr[0:1, 1:2], scr[0:1, 0:1], 1.0 / QMAX1, 1e-12,
                ALU.mult, ALU.max), reads=[scr], writes=[scr])
            P.op("vector", lambda e: e.reciprocal(scr[0:1, 2:3], scr[0:1, 1:2]),
                 reads=[scr], writes=[scr])
            for col, dstb in ((1, sxB), (2, ixB)):
                def fb(e, col=col):
                    return e.matmul(pP1[0:128, 0:1], C["ones1"][:],
                                    scr[0:1, col:col + 1], start=True, stop=True)
                P.op("tensor", fb, reads=[C["ones1"], scr], writes=[pP1])
                P.op("vector", lambda e, dstb=dstb: e.tensor_copy(
                    dstb[:], pP1[0:128, 0:1]), reads=[pP1], writes=[dstb])

        tpw4 = tap("wgt4", (128, NF * I1), BF16)
        if tpw4 is not None:
            P.dma("sync", lambda e: e.dma_start(out=tpw4, in_=wgT[:]),
                  reads=[wgT], writes=["taps"])
        if "nomemset" not in dbg:
            P.op("vector", lambda e: e.memset(pmg128[:], 0.0), writes=[pmg128])
            P.op("vector", lambda e: e.memset(pmg128b[:], 0.0), writes=[pmg128b])
            P.op("vector", lambda e: e.memset(pmu128[:], 0.0), writes=[pmu128])
            P.op("vector", lambda e: e.memset(pmu128b[:], 0.0), writes=[pmu128b])
        pmg_pp, pmg_n = [pmg128, pmg128b], [0]
        pmu_pp, pmu_n = [pmu128, pmu128b], [0]
        P.phase_barrier()

        xn1c = sb(nc, stGU, "xn1c", (128, NF * TC))
        xqc = sb(nc, stGU, "xqc", (128, NF * TC), BF16)
        tscr = sb(nc, stGU, "tscr", (128, TC))
        gob = sb(nc, stGU, "gob", (128, TC))
        rmx = sb(nc, stGU, "rmx", (128, 1))

        for c in range(nch):
            for a_ in range(NF):
                def flx(e, c=c, a_=a_):
                    return e.dma_start(
                        out=xn1c[:, a_ * TC:(a_ + 1) * TC],
                        in_=inp["xn1"][a_ * 128:(a_ + 1) * 128,
                                       c * TC:(c + 1) * TC])
                P.dma("sync", flx, writes=[xn1c])
            if "nolam" in dbg:
                lamiter = ()
            else:
                lamiter = ((ulam, WlamT), (Bxs, WBT))
            for dst, wsb in lamiter:
                for ic in range(NF):
                    def fm(e, ic=ic, wsb=wsb, c=c):
                        return e.matmul(pP0[0:16, 0:TC],
                                        wsb[:, ic * 16:(ic + 1) * 16],
                                        xn1c[:, ic * TC:(ic + 1) * TC],
                                        start=(ic == 0), stop=(ic == NF - 1))
                    P.op("tensor", fm, reads=[wsb, xn1c], writes=[pP0])
                if dst is ulam:
                    P.op("vector", lambda e, c=c: e.tensor_scalar(
                        ulam[0:16, c * TC:(c + 1) * TC], pP0[0:16, 0:TC],
                        blam[0:16, 0:1], None, ALU.add),
                        reads=[pP0, blam], writes=[ulam])
                else:
                    P.op("vector", lambda e, c=c: e.tensor_copy(
                        Bxs[0:16, c * TC:(c + 1) * TC], pP0[0:16, 0:TC]),
                        reads=[pP0], writes=[Bxs])
            for fi in range(0 if "noxq" in dbg else NF):
                sl = slice(fi * TC, (fi + 1) * TC)
                P.op("vector", lambda e, sl=sl: e.tensor_scalar(
                    xqc[:, sl].bitcast(F32)[:, 0:TC] if False else tscr[:],
                    xn1c[:, sl], ixB[:, 0:1], MAGIC, ALU.mult, ALU.add),
                    reads=[xn1c, ixB], writes=[tscr])
                P.op("vector", lambda e, sl=sl: e.tensor_scalar(
                    tscr[:], tscr[:], MAGIC, None, ALU.subtract),
                    reads=[tscr], writes=[tscr])
                P.op("vector", lambda e, sl=sl: e.tensor_scalar(
                    xqc[:, sl], tscr[:], sxB[:, 0:1], None, ALU.mult),
                    reads=[tscr, sxB], writes=[xqc])
            guiter = () if "nogu" in dbg else (
                ((wgT, out["gate_sp"], pmg_pp, pmg_n),) if "noup" in dbg else (
                (wuT, out["up_sp"], pmu_pp, pmu_n),
                (wgT, out["gate_sp"], pmg_pp, pmg_n)))
            for wsb, osp, pp_pp, pp_n in guiter:
                if wsb is wgT and c == 0:
                    tpw3 = tap("wgt3", (128, NF * I1), BF16)
                    if tpw3 is not None:
                        P.dma("sync", lambda e: e.dma_start(out=tpw3, in_=wgT[:]),
                              reads=[wgT], writes=["taps"])
                for oc in range(NOC1):
                    pp = pP0 if oc % 2 == 0 else pP1
                    for ic in range(NF):
                        def fm(e, ic=ic, oc=oc, wsb=wsb, pp=pp):
                            return e.matmul(
                                pp[:, 0:TC],
                                wsb[:, ic * I1 + oc * 128: ic * I1 + (oc + 1) * 128],
                                xqc[:, ic * TC:(ic + 1) * TC],
                                start=(ic == 0), stop=(ic == NF - 1))
                        P.op("tensor", fm, reads=[wsb, xqc], writes=[pp])
                    P.op("vector", lambda e, pp=pp: e.tensor_copy(gob[:], pp[:, 0:TC]),
                         reads=[pp], writes=[gob])
                    def fs(e, oc=oc, c=c, osp=osp):
                        return e.dma_start(
                            out=osp[oc * 128:(oc + 1) * 128, c * TC:(c + 1) * TC],
                            in_=gob[:])
                    P.dma("sync", fs, reads=[gob], writes=["out_" + osp.name])
                    P.op("vector", lambda e: e.tensor_reduce(
                        rmx[:], gob[:], AX.X, ALU.max, apply_absolute_value=True),
                        reads=[gob], writes=[rmx])
                    a0, a1 = pp_pp[pp_n[0] % 2], pp_pp[(pp_n[0] + 1) % 2]
                    pp_n[0] += 1
                    P.op("vector", lambda e, a0=a0, a1=a1: e.tensor_max(
                        a1[:], a0[:], rmx[:]), reads=[a0, rmx], writes=[a1])

        P.phase_barrier()
        stGU.close()
        stM = st.enter_context(ExitStack())
        p2x = sb(nc, stM, "xn1c2", (128, NF * TC))
        xn1b = sb(nc, stM, "xn1b", (128, NF * TC), BF16)
        p2t = sb(nc, stM, "tscr2", (128, TC))
        g1s = sb(nc, stM, "g1s", (128, TC))
        ygc = sb(nc, stM, "ygc", (128, NF * TC))
        p2g = sb(nc, stM, "gob2", (128, TC))

        _sigmoid_ops(P, lam[0:16, :], lam, ulam[0:16, :], ulam, t16[0:16, :], t16)
        P.op("vector", lambda e: e.tensor_tensor_scan(
            hsS[0:16, :], lam[0:16, :], Bxs[0:16, :], 0.0, ALU.mult, ALU.add),
            reads=[lam, Bxs], writes=[hsS])
        if "hs" in dbg:
            tp = tap("hs", (16, T))
            P.dma("sync", lambda e: e.dma_start(out=tp, in_=hsS[0:16, :]),
                  reads=[hsS], writes=["taps"])

        for c in range(nch):
            for a_ in range(NF):
                def flx(e, c=c, a_=a_):
                    return e.dma_start(
                        out=p2x[:, a_ * TC:(a_ + 1) * TC],
                        in_=inp["xn1"][a_ * 128:(a_ + 1) * 128,
                                       c * TC:(c + 1) * TC])
                P.dma("sync", flx, writes=[p2x])
            for a_ in range(NF):
                P.op("vector", lambda e, a_=a_: e.tensor_copy(
                    xn1b[:, a_ * TC:(a_ + 1) * TC],
                    p2x[:, a_ * TC:(a_ + 1) * TC]),
                    reads=[p2x], writes=[xn1b])
            for oc in range(NF):
                pg = pP2
                for ic in range(NF):
                    def fg(e, ic=ic, oc=oc):
                        return e.matmul(
                            pg[:, 0:TC],
                            WgT[:, ic * 1024 + oc * 128: ic * 1024 + (oc + 1) * 128],
                            xn1b[:, ic * TC:(ic + 1) * TC],
                            start=(ic == 0), stop=(ic == NF - 1))
                    P.op("tensor", fg, reads=[WgT, xn1b], writes=[pg])
                _sigmoid_ops(P, g1s[:], g1s, pg[:, 0:TC], pg, p2t[:], p2t)
                py = pP0 if oc % 2 == 0 else pP1
                def fy(e, oc=oc, c=c, py=py):
                    return e.matmul(py[:, 0:TC],
                                    WCT[0:16, oc * 128:(oc + 1) * 128],
                                    hsS[0:16, c * TC:(c + 1) * TC],
                                    start=True, stop=True)
                P.op("tensor", fy, reads=[WCT, hsS], writes=[py])
                P.op("vector", lambda e, oc=oc, py=py: e.tensor_tensor(
                    ygc[:, oc * TC:(oc + 1) * TC], py[:, 0:TC], g1s[:],
                    ALU.mult), reads=[py, g1s], writes=[ygc])
            for oc in range(NF):
                P.op("scalar", lambda e, oc=oc: e.activation(
                    p2t[:], ygc[:, oc * TC:(oc + 1) * TC], AF.Square),
                    reads=[ygc], writes=[p2t])
                def fss(e, oc=oc):
                    return e.matmul(pSt[0:1, 0:TC], C["onesk"][:], p2t[:],
                                    start=(oc == 0), stop=(oc == NF - 1))
                P.op("tensor", fss, reads=[C["onesk"], p2t], writes=[pSt])
            P.op("vector", lambda e: e.tensor_scalar(
                prow[0:1, :], pSt[0:1, 0:TC], 1.0 / 1024.0, 1e-6,
                ALU.mult, ALU.add), reads=[pSt], writes=[prow])
            P.op("scalar", lambda e: e.activation(prow[0:1, :], prow[0:1, :],
                                                  AF.Ln),
                 reads=[prow], writes=[prow])
            P.op("scalar", lambda e: e.activation(prow[0:1, :], prow[0:1, :],
                                                  AF.Exp, scale=-0.5),
                 reads=[prow], writes=[prow])
            def fbr(e):
                return e.matmul(pSt[:, 512:1024], C["ones1"][:], prow[0:1, :],
                                start=True, stop=True)
            P.op("tensor", fbr, reads=[C["ones1"], prow], writes=[pSt])
            for oc in range(NF):
                P.op("vector", lambda e, oc=oc: e.scalar_tensor_tensor(
                    p2g[:], ygc[:, oc * TC:(oc + 1) * TC], rmsw[:, oc:oc + 1],
                    pSt[:, 512:1024], ALU.mult, ALU.mult),
                    reads=[ygc, rmsw, pSt], writes=[p2g])
                def fo(e, c=c, oc=oc):
                    return e.dma_start(
                        out=out["attn1"][oc * 128:(oc + 1) * 128,
                                         c * TC:(c + 1) * TC], in_=p2g[:])
                P.dma("sync", fo, reads=[p2g], writes=["out_attn1"])

        for pp_pp, pp_n, oname in ((pmg_pp, pmg_n, "pmg"), (pmu_pp, pmu_n, "pmu")):
            fin = pp_pp[pp_n[0] % 2]
            _transpose128(P, C, pP0[0:1, 0:128], fin[:, 0:1], pP0, fin)
            P.op("vector", lambda e: e.tensor_reduce(
                scr[0:1, 3:4], pP0[0:1, 0:128], AX.X, ALU.max),
                reads=[pP0], writes=[scr])
            P.dma("sync", (lambda oname: (lambda e: e.dma_start(
                out=out[oname], in_=scr[0:1, 3:4])))(oname),
                reads=[scr], writes=["out_" + oname])

        P.barrier_all(["out_attn1", "out_gate_sp", "out_up_sp", "out_pmg",
                       "out_pmu", "taps"])
        with nc.Block() as block:
            P.emit(block, st)
    return nc, dbg_out



def _scalar_from_partials(nc, P, C, scr, wt8, pP0, pP1, pm_ap, qdiv, outs):
    def flp(e):
        return e.dma_start(out=wt8[0:8, 0:1], in_=pm_ap)
    P.dma("sync", flp, writes=[wt8])

    def ftp(e):
        return e.transpose(pP0[0:1, 0:8], wt8[0:8, 0:1], C["ident"][0:8, 0:8])
    P.op("tensor", ftp, reads=[wt8, C["ident"]], writes=[pP0])
    P.op("vector", lambda e: e.tensor_reduce(
        scr[0:1, 0:1], pP0[0:1, 0:8], AX.X, ALU.max), reads=[pP0], writes=[scr])
    P.op("vector", lambda e: e.tensor_scalar(
        scr[0:1, 1:2], scr[0:1, 0:1], 1.0 / qdiv, 1e-12, ALU.mult, ALU.max),
        reads=[scr], writes=[scr])
    P.op("vector", lambda e: e.reciprocal(scr[0:1, 2:3], scr[0:1, 1:2]),
         reads=[scr], writes=[scr])
    sB, iB = outs
    for col, dstb in ((1, sB), (2, iB)):
        def fb(e, col=col):
            return e.matmul(pP1[0:128, 0:1], C["ones1"][:],
                            scr[0:1, col:col + 1], start=True, stop=True)
        P.op("tensor", fb, reads=[C["ones1"], scr], writes=[pP1])
        P.op("vector", lambda e, dstb=dstb: e.tensor_copy(dstb[:], pP1[0:128, 0:1]),
             reads=[pP1], writes=[dstb])


def _finmax(nc, P, C, pP0, acc_fin, scr_ap, scr_buf, out_ap, token):
    _transpose128(P, C, pP0[0:1, 0:128], acc_fin[:, 0:1], pP0, acc_fin)
    P.op("vector", lambda e: e.tensor_reduce(
        scr_ap, pP0[0:1, 0:128], AX.X, ALU.max), reads=[pP0], writes=[scr_buf])
    P.dma("sync", lambda e: e.dma_start(out=out_ap, in_=scr_ap),
          reads=[scr_buf], writes=[token])



def build_k3(nch=NCH, dbg=()):
    nc = bass.Bass("TRN2", target_bir_lowering=False)
    gate_sp = nc.dram_tensor("gate_sp", [I1, T], F32, kind="ExternalInput").ap()
    pmg8 = nc.dram_tensor("pmg8", [8, 1], F32, kind="ExternalInput").ap()
    t_sp = nc.dram_tensor("t_sp", [I1, T], F32, kind="ExternalOutput").ap()
    pm2 = nc.dram_tensor("pm2", [1, 1], F32, kind="ExternalOutput").ap()
    P = Prog(nc)
    NB = 8
    W = NB * TC
    NPG = (I1 // 128) // NB
    with ExitStack() as st:
        C = _setup_consts(nc, P, st)
        pP0 = ps(nc, st, "pP0", (128, 512))
        pP1 = ps(nc, st, "pP1", (128, 512))
        scr = sb(nc, st, "scr", (1, 8))
        wt8 = sb(nc, st, "wt8", (8, 1))
        sgB = sb(nc, st, "sgB", (128, 1))
        igB = sb(nc, st, "igB", (128, 1))
        sgB2 = sb(nc, st, "sgB2", (128, 1))
        acc = [sb(nc, st, f"acc{i}", (128, 1)) for i in range(2)]
        rmx = sb(nc, st, "rmx", (128, 1))
        gS = [sb(nc, st, f"g{s}", (128, W)) for s in range(2)]
        a1S = [sb(nc, st, f"a1_{s}", (128, W)) for s in range(2)]
        a2S = [sb(nc, st, f"a2_{s}", (128, W)) for s in range(2)]
        a3S = [sb(nc, st, f"a3_{s}", (128, W)) for s in range(2)]
        _scalar_from_partials(nc, P, C, scr, wt8, pP0, pP1, pmg8, QMAX1,
                              (sgB, igB))
        P.op("vector", lambda e: e.tensor_scalar_mul(
            sgB2[:], sgB[:], 1.0 / 65536.0), reads=[sgB], writes=[sgB2])
        P.op("vector", lambda e: e.memset(acc[0][:], 0.0), writes=[acc[0]])
        P.op("vector", lambda e: e.memset(acc[1][:], 0.0), writes=[acc[1]])
        nacc = [0]
        passes = [(c, rg) for c in range(nch) for rg in range(NPG)]
        for p0 in range(0, len(passes), 2):
            pair = passes[p0:p0 + 2]
            for s, (c, rg) in enumerate(pair):
                g = gS[s]
                for j in range(NB):
                    def fl(e, c=c, oc=rg * NB + j, j=j, g=g):
                        return e.dma_start(
                            out=g[:, j * TC:(j + 1) * TC],
                            in_=gate_sp[oc * 128:(oc + 1) * 128,
                                        c * TC:(c + 1) * TC])
                    P.dma("sync", fl, writes=[g])
            for s in range(len(pair)):
                g, a1 = gS[s], a1S[s]
                P.op("scalar", lambda e, g=g, a1=a1: e.activation(
                    a1[:], g[:], AF.Identity, scale=igB[:, 0:1]),
                    reads=[g, igB], writes=[a1])
            for s in range(len(pair)):
                a1 = a1S[s]
                P.op("vector", lambda e, a1=a1: e.tensor_scalar(
                    a1[:], a1[:], MAGIC, MAGIC, ALU.add, ALU.subtract),
                    reads=[a1], writes=[a1])
            for s in range(len(pair)):
                a1, a2 = a1S[s], a2S[s]
                P.op("scalar", lambda e, a1=a1, a2=a2: e.activation(
                    a2[:], a1[:], AF.Identity, scale=sgB2[:, 0:1]),
                    reads=[a1, sgB2], writes=[a2])
            for s in range(len(pair)):
                a2, a3 = a2S[s], a3S[s]
                P.op("scalar", lambda e, a2=a2, a3=a3: e.activation(
                    a3[:], a2[:], AF.Sigmoid, scale=65536.0),
                    reads=[a2], writes=[a3])
            for s in range(len(pair)):
                a1, a3 = a1S[s], a3S[s]
                P.op("vector", lambda e, a1=a1, a3=a3: e.tensor_scalar(
                    a1[:], a3[:], 65536.0, MAGIC, ALU.mult, ALU.add),
                    reads=[a3], writes=[a1])
            for s in range(len(pair)):
                a1 = a1S[s]
                P.op("vector", lambda e, a1=a1: e.tensor_scalar(
                    a1[:], a1[:], MAGIC, 65535.0, ALU.subtract, ALU.min),
                    reads=[a1], writes=[a1])
            for s in range(len(pair)):
                a1, a2, a3 = a1S[s], a2S[s], a3S[s]
                P.op("vector", lambda e, a1=a1, a2=a2, a3=a3: e.tensor_tensor(
                    a3[:], a2[:], a1[:], ALU.mult),
                    reads=[a1, a2], writes=[a3])
            for s, (c, rg) in enumerate(pair):
                a3 = a3S[s]
                P.op("vector", lambda e, a3=a3: e.tensor_reduce(
                    rmx[:], a3[:], AX.X, ALU.max, apply_absolute_value=True),
                    reads=[a3], writes=[rmx])
                a0, a1_ = acc[nacc[0] % 2], acc[(nacc[0] + 1) % 2]
                nacc[0] += 1
                P.op("vector", lambda e, a0=a0, a1_=a1_: e.tensor_max(
                    a1_[:], a0[:], rmx[:]), reads=[a0, rmx], writes=[a1_])
                for j in range(NB):
                    def fs(e, c=c, oc=rg * NB + j, j=j, a3=a3):
                        return e.dma_start(
                            out=t_sp[oc * 128:(oc + 1) * 128,
                                     c * TC:(c + 1) * TC],
                            in_=a3[:, j * TC:(j + 1) * TC])
                    P.dma("gpsimd", fs, reads=[a3], writes=["out_t"])
        _finmax(nc, P, C, pP0, acc[nacc[0] % 2], scr[0:1, 3:4], scr, pm2,
                "out_pm2")
        P.barrier_all(["out_t", "out_pm2"])
        with nc.Block() as block:
            P.emit(block, st)
    return nc, {}



def build_k4(nch=NCH, dbg=()):
    nc = bass.Bass("TRN2", target_bir_lowering=False)
    t_sp = nc.dram_tensor("t_sp", [I1, T], F32, kind="ExternalInput").ap()
    up_sp = nc.dram_tensor("up_sp", [I1, T], F32, kind="ExternalInput").ap()
    pm28 = nc.dram_tensor("pm28", [8, 1], F32, kind="ExternalInput").ap()
    pmu8 = nc.dram_tensor("pmu8", [8, 1], F32, kind="ExternalInput").ap()
    o_sp = nc.dram_tensor("o_sp", [I1, T], F32, kind="ExternalOutput").ap()
    pm3 = nc.dram_tensor("pm3", [1, 1], F32, kind="ExternalOutput").ap()
    P = Prog(nc)
    NB = 8
    W = NB * TC
    NPG = (I1 // 128) // NB
    with ExitStack() as st:
        C = _setup_consts(nc, P, st)
        pP0 = ps(nc, st, "pP0", (128, 512))
        pP1 = ps(nc, st, "pP1", (128, 512))
        scr = sb(nc, st, "scr", (1, 8))
        wt8 = sb(nc, st, "wt8", (8, 1))
        s2B = sb(nc, st, "s2B", (128, 1))
        i2B = sb(nc, st, "i2B", (128, 1))
        suB = sb(nc, st, "suB", (128, 1))
        iuB = sb(nc, st, "iuB", (128, 1))
        acc = [sb(nc, st, f"acc{i}", (128, 1)) for i in range(2)]
        rmx = sb(nc, st, "rmx", (128, 1))
        tS = [sb(nc, st, f"t{s}", (128, W)) for s in range(2)]
        uS = [sb(nc, st, f"u{s}", (128, W)) for s in range(2)]
        a1S = [sb(nc, st, f"a1_{s}", (128, W)) for s in range(2)]
        a2S = [sb(nc, st, f"a2_{s}", (128, W)) for s in range(2)]
        a3S = [sb(nc, st, f"a3_{s}", (128, W)) for s in range(2)]
        _scalar_from_partials(nc, P, C, scr, wt8, pP0, pP1, pm28, QMAX1,
                              (s2B, i2B))
        _scalar_from_partials(nc, P, C, scr, wt8, pP0, pP1, pmu8, QMAX1,
                              (suB, iuB))
        P.op("vector", lambda e: e.memset(acc[0][:], 0.0), writes=[acc[0]])
        P.op("vector", lambda e: e.memset(acc[1][:], 0.0), writes=[acc[1]])
        nacc = [0]
        passes = [(c, rg) for c in range(nch) for rg in range(NPG)]
        for p0 in range(0, len(passes), 2):
            pair = passes[p0:p0 + 2]
            for s, (c, rg) in enumerate(pair):
                tb, ub = tS[s], uS[s]
                for j in range(NB):
                    def fl(e, c=c, oc=rg * NB + j, j=j, tb=tb):
                        return e.dma_start(
                            out=tb[:, j * TC:(j + 1) * TC],
                            in_=t_sp[oc * 128:(oc + 1) * 128,
                                     c * TC:(c + 1) * TC])
                    P.dma("sync", fl, writes=[tb])
                    def fl2(e, c=c, oc=rg * NB + j, j=j, ub=ub):
                        return e.dma_start(
                            out=ub[:, j * TC:(j + 1) * TC],
                            in_=up_sp[oc * 128:(oc + 1) * 128,
                                      c * TC:(c + 1) * TC])
                    P.dma("gpsimd", fl2, writes=[ub])
            for s in range(len(pair)):
                tb, a1 = tS[s], a1S[s]
                P.op("scalar", lambda e, tb=tb, a1=a1: e.activation(
                    a1[:], tb[:], AF.Identity, scale=i2B[:, 0:1]),
                    reads=[tb, i2B], writes=[a1])
            for s in range(len(pair)):
                a1 = a1S[s]
                P.op("vector", lambda e, a1=a1: e.tensor_scalar(
                    a1[:], a1[:], MAGIC, MAGIC, ALU.add, ALU.subtract),
                    reads=[a1], writes=[a1])
            for s in range(len(pair)):
                a1, a2 = a1S[s], a2S[s]
                P.op("scalar", lambda e, a1=a1, a2=a2: e.activation(
                    a2[:], a1[:], AF.Identity, scale=s2B[:, 0:1]),
                    reads=[a1, s2B], writes=[a2])
            for s in range(len(pair)):
                ub, a1 = uS[s], a1S[s]
                P.op("scalar", lambda e, ub=ub, a1=a1: e.activation(
                    a1[:], ub[:], AF.Identity, scale=iuB[:, 0:1]),
                    reads=[ub, iuB], writes=[a1])
            for s in range(len(pair)):
                a1 = a1S[s]
                P.op("vector", lambda e, a1=a1: e.tensor_scalar(
                    a1[:], a1[:], MAGIC, MAGIC, ALU.add, ALU.subtract),
                    reads=[a1], writes=[a1])
            for s in range(len(pair)):
                a1, a3 = a1S[s], a3S[s]
                P.op("scalar", lambda e, a1=a1, a3=a3: e.activation(
                    a3[:], a1[:], AF.Identity, scale=suB[:, 0:1]),
                    reads=[a1, suB], writes=[a3])
            for s in range(len(pair)):
                tb, a2, a3 = tS[s], a2S[s], a3S[s]
                P.op("vector", lambda e, tb=tb, a2=a2, a3=a3: e.tensor_tensor(
                    tb[:], a2[:], a3[:], ALU.mult),
                    reads=[a2, a3], writes=[tb])
            for s, (c, rg) in enumerate(pair):
                tb = tS[s]
                P.op("vector", lambda e, tb=tb: e.tensor_reduce(
                    rmx[:], tb[:], AX.X, ALU.max, apply_absolute_value=True),
                    reads=[tb], writes=[rmx])
                a0, a1_ = acc[nacc[0] % 2], acc[(nacc[0] + 1) % 2]
                nacc[0] += 1
                P.op("vector", lambda e, a0=a0, a1_=a1_: e.tensor_max(
                    a1_[:], a0[:], rmx[:]), reads=[a0, rmx], writes=[a1_])
                q = "sync" if (p0 // 2 + s) % 2 == 0 else "gpsimd"
                for j in range(NB):
                    def fs(e, c=c, oc=rg * NB + j, j=j, tb=tb):
                        return e.dma_start(
                            out=o_sp[oc * 128:(oc + 1) * 128,
                                     c * TC:(c + 1) * TC],
                            in_=tb[:, j * TC:(j + 1) * TC])
                    P.dma(q, fs, reads=[tb], writes=["out_o"])
        _finmax(nc, P, C, pP0, acc[nacc[0] % 2], scr[0:1, 3:4], scr, pm3,
                "out_pm3")
        P.barrier_all(["out_o", "out_pm3"])
        with nc.Block() as block:
            P.emit(block, st)
    return nc, {}



def build_k5(nch=NCH, dbg=()):
    nc = bass.Bass("TRN2", target_bir_lowering=False)
    o_sp = nc.dram_tensor("o_sp", [I1, T], F32, kind="ExternalInput").ap()
    pm38 = nc.dram_tensor("pm38", [8, 1], F32, kind="ExternalInput").ap()
    x1 = nc.dram_tensor("x1", [D, T], F32, kind="ExternalInput").ap()
    attn1 = nc.dram_tensor("attn1", [D, T], F32, kind="ExternalInput").ap()
    wd1 = nc.dram_tensor("wd1", [D, I1], F32, kind="ExternalInput").ap()
    OUT = nc.dram_tensor("OUT", [T, D], F32, kind="ExternalOutput").ap()
    NIC1 = I1 // 128
    P = Prog(nc)
    with ExitStack() as st:
        C = _setup_consts(nc, P, st)
        pP0 = ps(nc, st, "pP0", (128, 512))
        pP1 = ps(nc, st, "pP1", (128, 512))
        pT = ps(nc, st, "pT", (128, 1024))
        pT2 = ps(nc, st, "pT2", (128, 1024))
        pKT = ps(nc, st, "pKT", (128, 1024), BF16)
        pKT2 = ps(nc, st, "pKT2", (128, 1024), BF16)
        scr = sb(nc, st, "scr", (1, 8))
        wt8 = sb(nc, st, "wt8", (8, 1))
        s3B = sb(nc, st, "s3B", (128, 1))
        i3B = sb(nc, st, "i3B", (128, 1))
        wdT = sb(nc, st, "wdT", (128, NIC1 * 1024), BF16)

        with ExitStack() as stw:
            wtL = [sb(nc, stw, f"wt{s}", (128, I1)) for s in range(2)]
            wq = sb(nc, stw, "wq", (128, I1))
            bs = sb(nc, stw, "bs", (128, 96))
            br = sb(nc, stw, "br", (128, 96))
            qbL = [sb(nc, stw, f"qb{s}", (128, I1), BF16) for s in range(2)]

            def bview32(t, nb):
                return t[:].rearrange("p b -> p b ()").broadcast_to((128, nb, 32))

            for it in range(8):
                wt = wtL[it % 2]
                qb = qbL[it % 2]
                def fl(e, it=it, wt=wt):
                    return e.dma_start(out=wt[:],
                                       in_=wd1[it * 128:(it + 1) * 128, :])
                P.dma("sync", fl, writes=[wt])
                P.op("vector", lambda e, wt=wt: e.tensor_reduce(
                    bs[:], wt[:].rearrange("p (b k) -> p b k", k=32), AX.X,
                    ALU.max, apply_absolute_value=True), reads=[wt], writes=[bs])
                P.op("vector", lambda e: e.tensor_scalar(
                    bs[:], bs[:], 1.0 / 7.0, 1e-12, ALU.mult, ALU.max),
                    reads=[bs], writes=[bs])
                P.op("vector", lambda e: e.reciprocal(br[:], bs[:]),
                     reads=[bs], writes=[br])
                P.op("vector", lambda e, wt=wt: e.tensor_tensor(
                    wq[:].rearrange("p (b k) -> p b k", k=32),
                    wt[:].rearrange("p (b k) -> p b k", k=32),
                    bview32(br, 96), ALU.mult), reads=[wt, br], writes=[wq])
                P.op("vector", lambda e: e.tensor_scalar_add(
                    wq[:], wq[:], MAGIC), reads=[wq], writes=[wq])
                P.op("vector", lambda e: e.tensor_scalar(
                    wq[:], wq[:], MAGIC, 7.0, ALU.subtract, ALU.min),
                    reads=[wq], writes=[wq])
                P.op("vector", lambda e: e.tensor_scalar_max(
                    wq[:], wq[:], -8.0), reads=[wq], writes=[wq])
                P.op("vector", lambda e, qb=qb: e.tensor_tensor(
                    qb[:].rearrange("p (b k) -> p b k", k=32),
                    wq[:].rearrange("p (b k) -> p b k", k=32),
                    bview32(bs, 96), ALU.mult), reads=[wq, bs], writes=[qb])
                for g in range(NIC1 // 8):
                    pkx = pKT if (it * 3 + g) % 2 == 0 else pKT2
                    for k in range(8):
                        def ft(e, g=g, k=k, qb=qb, pkx=pkx):
                            return e.transpose(
                                pkx[:, k * 128:(k + 1) * 128],
                                qb[:, (g * 8 + k) * 128:(g * 8 + k + 1) * 128],
                                C["identb"][:])
                        P.op("tensor", ft, reads=[qb, C["identb"]], writes=[pkx])
                    def fe(e, g=g, it=it, pkx=pkx):
                        return e.tensor_copy(
                            wdT[:, (it * NIC1 + g * 8) * 128:
                                (it * NIC1 + g * 8 + 8) * 128],
                            pkx[:, 0:1024])
                    P.op("vector", fe, reads=[pkx], writes=[wdT])
            _scalar_from_partials(nc, P, C, scr, wt8, pP0, pP1, pm38, QMAX1,
                                  (s3B, i3B))
        P.phase_barrier()

        stM = st.enter_context(ExitStack())
        ob = sb(nc, stM, "ob", (128, NIC1 * TC), BF16)
        ow = sb(nc, stM, "ow", (128, NIC1 * TC))
        x1c = sb(nc, stM, "x1c", (128, NF * TC))
        a1c = sb(nc, stM, "a1c", (128, NF * TC))
        x2c = sb(nc, stM, "x2c", (128, NF * TC))
        hb = sb(nc, stM, "hb", (128, TC))
        otmL = [sb(nc, stM, f"otm{s}", (128, 1024)) for s in range(2)]

        for c in range(nch):
            for a_ in range(NF):
                def flx(e, c=c, a_=a_):
                    return e.dma_start(
                        out=x1c[:, a_ * TC:(a_ + 1) * TC],
                        in_=x1[a_ * 128:(a_ + 1) * 128, c * TC:(c + 1) * TC])
                P.dma("gpsimd", flx, writes=[x1c])
                def fla(e, c=c, a_=a_):
                    return e.dma_start(
                        out=a1c[:, a_ * TC:(a_ + 1) * TC],
                        in_=attn1[a_ * 128:(a_ + 1) * 128, c * TC:(c + 1) * TC])
                P.dma("gpsimd", fla, writes=[a1c])
            for oc in range(NIC1):
                def flo(e, c=c, oc=oc):
                    return e.dma_start(
                        out=ow[:, oc * TC:(oc + 1) * TC],
                        in_=o_sp[oc * 128:(oc + 1) * 128, c * TC:(c + 1) * TC])
                P.dma("sync", flo, writes=[ow])
            P.op("vector", lambda e: e.tensor_scalar(
                ow[:], ow[:], i3B[:, 0:1], MAGIC, ALU.mult, ALU.add),
                reads=[ow, i3B], writes=[ow])
            P.op("vector", lambda e: e.tensor_scalar(
                ob[:], ow[:], MAGIC, s3B[:, 0:1], ALU.subtract, ALU.mult),
                reads=[ow, s3B], writes=[ob])
            for oc in range(NF):
                pp = pP0 if oc % 2 == 0 else pP1
                for ic in range(NIC1):
                    def fm(e, ic=ic, oc=oc, pp=pp):
                        return e.matmul(
                            pp[:, 0:TC],
                            wdT[:, (oc * NIC1 + ic) * 128:
                                (oc * NIC1 + ic + 1) * 128],
                            ob[:, ic * TC:(ic + 1) * TC],
                            start=(ic == 0), stop=(ic == NIC1 - 1))
                    P.op("tensor", fm, reads=[wdT, ob], writes=[pp])
                P.op("vector", lambda e, oc=oc, pp=pp: e.tensor_tensor(
                    hb[:], pp[:, 0:TC], x1c[:, oc * TC:(oc + 1) * TC], ALU.add),
                    reads=[pp, x1c], writes=[hb])
                P.op("vector", lambda e, oc=oc: e.tensor_tensor(
                    x2c[:, oc * TC:(oc + 1) * TC], hb[:],
                    a1c[:, oc * TC:(oc + 1) * TC], ALU.add),
                    reads=[hb, a1c], writes=[x2c])
            for sc in range(NSC):
                ptx = pT if sc % 2 == 0 else pT2
                otm = otmL[sc % 2]
                for oc in range(NF):
                    def ftx(e, sc=sc, oc=oc, ptx=ptx):
                        return e.transpose(
                            ptx[:, oc * 128:(oc + 1) * 128],
                            x2c[:, oc * TC + sc * 128: oc * TC + (sc + 1) * 128],
                            C["ident"][:])
                    P.op("tensor", ftx, reads=[x2c, C["ident"]], writes=[ptx])
                P.op("vector", lambda e, ptx=ptx, otm=otm: e.tensor_copy(
                    otm[:], ptx[:, 0:1024]), reads=[ptx], writes=[otm])
                def fso(e, c=c, sc=sc, otm=otm):
                    return e.dma_start(
                        out=OUT[c * TC + sc * 128: c * TC + (sc + 1) * 128, :],
                        in_=otm[:])
                P.dma("gpsimd", fso, reads=[otm], writes=["out_final"])
        P.barrier_all(["out_final"])
        with nc.Block() as block:
            P.emit(block, st)
    return nc, {}



def _np(x):
    return np.ascontiguousarray(np.asarray(x))


def kernel(**inputs):
    i = inputs
    x0 = np.asarray(i["L0_attn_x"])

    nc1, _ = build_k1()
    base1 = {
        "ln_g": _np(i["L0_attn_ln_g"]), "ln_b": _np(i["L0_attn_ln_b"]),
        "Wx": _np(i["L0_attn_Wx"]), "Ww": _np(i["L0_attn_Ww"]),
        "bw": _np(i["L0_attn_bw"]), "Wk": _np(i["L0_attn_Wk"]),
        "Wv": _np(i["L0_attn_Wv"]), "Wr": _np(i["L0_attn_Wr"]),
        "Wo": _np(i["L0_attn_Wo"]), "wg0": _np(i["L0_ffn_w_gate"]),
        "wu0": _np(i["L0_ffn_w_up"]), "wd0": _np(i["L0_ffn_w_down"]),
    }
    in1 = [dict(base1, x0=_np(x0[b])) for b in range(8)]
    r1 = run_bass_kernel_spmd(nc1, in1, core_ids=list(range(8)))
    pmx8 = np.concatenate([_np(r1.results[b]["pmx"]) for b in range(8)],
                          axis=0).astype(np.float32)
    x1s = [_np(r1.results[b]["x1"]) for b in range(8)]
    xn1s = [_np(r1.results[b]["xn1"]) for b in range(8)]

    nc2, _ = build_k2()
    base2 = {
        "pmx8": pmx8,
        "W_lambda": _np(i["L1_attn_W_lambda"]), "b_lambda": _np(i["L1_attn_b_lambda"]),
        "W_B": _np(i["L1_attn_W_B"]), "W_C": _np(i["L1_attn_W_C"]),
        "W_gate": _np(i["L1_attn_W_gate"]), "rms_weight": _np(i["L1_attn_rms_weight"]),
        "wg1": _np(i["L1_ffn_w_gate"]), "wu1": _np(i["L1_ffn_w_up"]),
    }
    in2 = [dict(base2, xn1=xn1s[b]) for b in range(8)]
    r2 = run_bass_kernel_spmd(nc2, in2, core_ids=list(range(8)))
    pmg8 = np.concatenate([_np(r2.results[b]["pmg"]) for b in range(8)],
                          axis=0).astype(np.float32)
    pmu8 = np.concatenate([_np(r2.results[b]["pmu"]) for b in range(8)],
                          axis=0).astype(np.float32)

    nc3, _ = build_k3()
    in3 = [dict(gate_sp=_np(r2.results[b]["gate_sp"]), pmg8=pmg8)
           for b in range(8)]
    r3 = run_bass_kernel_spmd(nc3, in3, core_ids=list(range(8)))
    pm28 = np.concatenate([_np(r3.results[b]["pm2"]) for b in range(8)],
                          axis=0).astype(np.float32)

    nc4, _ = build_k4()
    in4 = [dict(t_sp=_np(r3.results[b]["t_sp"]),
                up_sp=_np(r2.results[b]["up_sp"]), pm28=pm28, pmu8=pmu8)
           for b in range(8)]
    r4 = run_bass_kernel_spmd(nc4, in4, core_ids=list(range(8)))
    pm38 = np.concatenate([_np(r4.results[b]["pm3"]) for b in range(8)],
                          axis=0).astype(np.float32)

    nc5, _ = build_k5()
    in5 = [dict(o_sp=_np(r4.results[b]["o_sp"]), pm38=pm38, x1=x1s[b],
                attn1=_np(r2.results[b]["attn1"]), wd1=_np(i["L1_ffn_w_down"]))
           for b in range(8)]
    r5 = run_bass_kernel_spmd(nc5, in5, core_ids=list(range(8)))
    out = np.stack([_np(r5.results[b]["OUT"]) for b in range(8)], axis=0)
    return out.astype(np.float32)
